# version 04
# baseline (speedup 1.0000x reference)
import contextlib
import numpy as np
import concourse.bass as bass
import concourse.mybir as mybir
from concourse.bass_utils import run_bass_kernel_spmd

F32 = mybir.dt.float32
BF16 = mybir.dt.bfloat16
AF = mybir.ActivationFunctionType
ALU = mybir.AluOpType
AX = mybir.AxisListType

NCORES = 8
B_GLOBAL = 16
D = 1024
DS = 256
I = 2816
BPC = B_GLOBAL // NCORES
MID = (0.25, 0.75, 1.25, 1.75, 2.5, 3.5, 5.0)
EPS = 1e-5
KD = D // 128
KS = DS // 128
KI = I // 128

ENGINES = ("sync", "vector", "scalar", "tensor", "gpsimd")


class Sem:
    def __init__(self, nc, es, name):
        self.h = es.enter_context(nc.semaphore(name))
        self.name = name
        self.n = 0


class Prog:
    def __init__(self):
        self.ops = {e: [] for e in ENGINES}
        self.waited = {e: {} for e in ENGINES}

    def emit(self, eng, fn):
        self.ops[eng].append(fn)

    def wait(self, eng, sem, val):
        if val is None or val <= 0:
            return
        if self.waited[eng].get(sem.name, 0) >= val:
            return
        self.waited[eng][sem.name] = val
        self.ops[eng].append(lambda e, s=sem.h, v=val: e.wait_ge(s, v))


def build_nc(T=4096, debug=False):
    TOK = BPC * T
    CH = 512
    NMT = CH // 128
    NCH = TOK // CH
    NS = 512
    SLICES = [(s, min(NS, I - s)) for s in range(0, I, NS)]
    W = 256
    NW = T // W

    nc = bass.Bass("TRN2", target_bir_lowering=False, debug=False)
    es = contextlib.ExitStack()

    dx = nc.dram_tensor("x", [TOK, D], F32, kind="ExternalInput").ap()
    dwin = nc.dram_tensor("w_in", [D, DS], F32, kind="ExternalInput").ap()
    dbin = nc.dram_tensor("b_in", [1, DS], F32, kind="ExternalInput").ap()
    dwgm = nc.dram_tensor("w_gate_m", [D, D], F32, kind="ExternalInput").ap()
    dbgm = nc.dram_tensor("b_gate_m", [1, D], F32, kind="ExternalInput").ap()
    dam = nc.dram_tensor("a_m", [DS, DS], F32, kind="ExternalInput").ap()
    dbm = nc.dram_tensor("bm_m", [DS, DS], F32, kind="ExternalInput").ap()
    dct = nc.dram_tensor("c_t", [DS, D], F32, kind="ExternalInput").ap()
    ddt = nc.dram_tensor("d_t", [DS, D], F32, kind="ExternalInput").ap()
    dwo = nc.dram_tensor("w_out", [D, D], F32, kind="ExternalInput").ap()
    dbo = nc.dram_tensor("b_out", [1, D], F32, kind="ExternalInput").ap()
    dwg = nc.dram_tensor("ffn_wg", [I, D], F32, kind="ExternalInput").ap()
    dwu = nc.dram_tensor("ffn_wu", [I, D], F32, kind="ExternalInput").ap()
    dwd = nc.dram_tensor("ffn_wd", [D, I], F32, kind="ExternalInput").ap()
    dy = nc.dram_tensor("y", [TOK, D], F32, kind="ExternalOutput").ap()
    _ik = "ExternalOutput" if debug else "Internal"
    dx1 = nc.dram_tensor("x1s", [TOK, D], F32, kind=_ik).ap()
    dx2 = nc.dram_tensor("x2s", [TOK, D], F32, kind=_ik).ap()
    dxin = nc.dram_tensor("xins", [TOK, D], F32, kind=_ik).ap()
    dg = nc.dram_tensor("gs", [TOK, D], F32, kind=_ik).ap()
    du = nc.dram_tensor("us", [TOK, DS], BF16, kind=_ik).ap()
    dwgq = nc.dram_tensor("wgqT", [KD, 128, I], BF16, kind=_ik).ap()
    dwuq = nc.dram_tensor("wuqT", [KD, 128, I], BF16, kind=_ik).ap()
    dwdq = nc.dram_tensor("wdqT", [KI, 128, D], BF16, kind=_ik).ap()
    dxq = nc.dram_tensor("xqdbg", [TOK, D], BF16, kind=_ik).ap() if debug else None
    dhq = nc.dram_tensor("hqdbg", [TOK, I], BF16, kind=_ik).ap() if debug else None
    dqt = nc.dram_tensor("qtdbg", [TOK, D], BF16, kind=_ik).ap() if debug else None
    dup = nc.dram_tensor("updbg", [TOK, DS], F32, kind=_ik).ap() if debug else None
    dwdbg = nc.dram_tensor("wdbg", [128, KD * DS], BF16, kind=_ik).ap() if debug else None
    dht = nc.dram_tensor("htdbg", [128, 4 * T], BF16, kind=_ik).ap() if debug else None
    dvb = nc.dram_tensor("vbdbg", [128, 1024], F32, kind=_ik).ap() if debug else None

    def sb(name, shape, dt=BF16):
        return es.enter_context(nc.sbuf_tensor(name, shape, dt))

    swin = sb("swin", [128, KD * DS])
    swgo = sb("swgo", [128, KD * D])
    sct = sb("sct", [128, KS * D])
    sdt = sb("sdt", [128, KS * D])
    sA = sb("sA", [128, 4 * 128])
    sBm = sb("sBm", [128, 4 * 128])
    sHT = sb("sHT", [128, 4 * T])
    swg = [sb(f"swg{i}", [128, KD * NS]) for i in range(2)]
    swu = [sb(f"swu{i}", [128, KD * NS]) for i in range(2)]
    swd = [sb(f"swd{i}", [128, NS]) for i in range(2)]
    sxa = sb("sxa", [128, NMT * D], F32)
    sxb_ = sb("sxb", [128, NMT * D], F32)
    sq_ = sb("sq", [128, NMT * D])
    sqT = sb("sqT", [128, KD * CH])
    su_ = sb("su", [128, NMT * DS])
    suT = sb("suT", [128, KS * CH])
    suTw = [sb(f"suTw{i}", [128, KS * 2 * W]) for i in range(2)]
    s_hqT = sb("s_hqT", [128, KI * CH])
    t_sil = sb("t_sil", [128, NS], F32)
    t_hq = sb("t_hq", [128, NS])
    t_a = sb("t_a", [128, D], F32)
    QB = [sb(f"qb{i}", [128, D]) for i in range(6)]
    s_bm = sb("s_bm", [128, D // 16], F32)
    s_ri = sb("s_ri", [128, D // 16], F32)
    s_sc = sb("s_sc", [128, D // 16], F32)
    s_bn = sb("s_bn", [128, 12], F32)
    s_mv = sb("s_mv", [128, 2], F32)
    s_rs = sb("s_rs", [128, 1], F32)
    s_nm = sb("s_nm", [128, 1], F32)
    s_ve = sb("s_ve", [128, 1], F32)
    s_one = sb("s_one", [1, 128], F32)
    s_bias = sb("s_bias", [1, D], F32)
    s_bin = sb("s_bin", [1, DS], F32)
    s_thr = sb("s_thr", [128, 8], F32)
    s_wld = sb("s_wld", [128, 1024], F32)
    s_wq = sb("s_wq", [128, 1024])
    s_tp = sb("s_tp", [128, 1024])

    pp = [es.enter_context(nc.psum_tensor(f"pp{i}", [128, 512], F32)) for i in range(8)]

    LD = Sem(nc, es, "LD")
    ST = Sem(nc, es, "ST")
    TW = [Sem(nc, es, f"TW{i}") for i in range(2)]
    TD = [Sem(nc, es, f"TD{i}") for i in range(2)]
    TQ = [Sem(nc, es, f"TQ{i}") for i in range(4)]
    TH = [Sem(nc, es, f"TH{i}") for i in range(4)]
    TU = [Sem(nc, es, f"TU{i}") for i in range(2)]
    TP = [Sem(nc, es, f"TP{i}") for i in range(2)]
    PE = Sem(nc, es, "PE")
    DV = Sem(nc, es, "DV")
    AC = Sem(nc, es, "AC")
    ACS = Sem(nc, es, "ACS")
    SG = Sem(nc, es, "SG")
    LX1 = Sem(nc, es, "LX1")
    LG = Sem(nc, es, "LG")
    SWL = Sem(nc, es, "SWL")
    ALL_DMA = [LD, ST, SG, LX1, LG, SWL] + TW + TD + TQ + TH + TU + TP

    prog = Prog()

    def dma(out, in_, waits=(), sem=None):
        sem = sem or LD
        for (s, v) in waits:
            prog.wait("sync", s, v)
        sem.n += 16
        prog.emit("sync", lambda e, o=out, i=in_, s=sem.h: e.dma_start(out=o, in_=i).then_inc(s, 16))
        return sem.n

    def dmat(out, in_, waits=(), sem=None):
        sem = sem or LD
        for (s, v) in waits:
            prog.wait("sync", s, v)
        sem.n += 16
        prog.emit("sync", lambda e, o=out, i=in_, s=sem.h: e.dma_start_transpose(o, i).then_inc(s, 16))
        return sem.n

    def join_all(full=False):
        sems = ALL_DMA if full else [LD, ST, SG, LX1, LG]
        for s in sems:
            prog.wait("sync", s, s.n)
        prog.wait("sync", PE, PE.n)
        prog.wait("sync", DV, DV.n)
        prog.wait("sync", AC, AC.n)
        prog.wait("sync", ACS, ACS.n)

    def _op(eng, sem, fn, waits):
        for (s, v) in waits:
            prog.wait(eng, s, v)
        sem.n += 1
        prog.emit(eng, lambda e, fn=fn, s=sem.h: fn(e).then_inc(s, 1))
        return sem.n

    def vec(fn, waits=()):
        return _op("vector", DV, fn, waits)

    def act(fn, waits=()):
        return _op("scalar", AC, fn, waits)

    def pe(fn, waits=(), inc=True, sem=None):
        for (s, v) in waits:
            prog.wait("tensor", s, v)
        if inc:
            s = sem or PE
            s.n += 1
            prog.emit("tensor", lambda e, fn=fn, sh=s.h: fn(e).then_inc(sh, 1))
            return s.n
        prog.emit("tensor", fn)
        return None

    def quantize(src, dst, F, src_ready):
        NB = F // 16
        prog.wait("vector", AC, AC.n)
        prog.wait("scalar", DV, DV.n)
        for s in TH + TQ:
            prog.wait("vector", s, s.n)
        w = list(src_ready)
        mbm = vec(lambda e: e.tensor_reduce(s_bm[:, :NB],
                                            src.rearrange("p (a b) -> p a b", b=16),
                                            AX.X, ALU.max, apply_absolute_value=True), w)
        mbm = vec(lambda e: e.tensor_scalar(s_bm[:, :NB], s_bm[:, :NB], 6e-12, None, ALU.max),
                  [(DV, mbm)])
        mri = vec(lambda e: e.reciprocal(s_ri[:, :NB], s_bm[:, :NB]), [(DV, mbm)])
        mri = vec(lambda e: e.tensor_scalar(s_ri[:, :NB], s_ri[:, :NB], 6.0, None, ALU.mult),
                  [(DV, mri)])
        msc = vec(lambda e: e.tensor_scalar(s_sc[:, :NB], s_bm[:, :NB], 1.0 / 12.0, None,
                                            ALU.mult), [(DV, mbm)])
        mah = act(lambda e: e.activation(t_a[:, :F], src, AF.Abs), w)
        rb = s_ri[:, :NB].rearrange("p (a o) -> p a o", o=1).broadcast_to([128, NB, 16])
        ma = vec(lambda e: e.tensor_tensor(t_a[:, :F].rearrange("p (a b) -> p a b", b=16),
                                           t_a[:, :F].rearrange("p (a b) -> p a b", b=16),
                                           rb, ALU.mult), [(DV, mri), (AC, mah)])
        aw = [(DV, ma)]
        m1 = vec(lambda e: e.tensor_scalar(QB[0][:, :F], t_a[:, :F], MID[0], None, ALU.is_gt), aw)
        m2 = vec(lambda e: e.tensor_scalar(QB[1][:, :F], t_a[:, :F], MID[1], None, ALU.is_gt), aw)
        mc = [act(lambda e, i=i, j=j: e.activation(QB[j][:, :F], t_a[:, :F], AF.Sign,
                                                   bias=s_thr[:, i:i + 1]), aw)
              for i, j in ((2, 2), (3, 3), (4, 4), (5, 5))]
        m12 = vec(lambda e: e.tensor_add(QB[0][:, :F], QB[0][:, :F], QB[1][:, :F]), [(DV, m2)])
        m34 = vec(lambda e: e.tensor_add(QB[2][:, :F], QB[2][:, :F], QB[3][:, :F]),
                  [(AC, mc[1])])
        m56 = vec(lambda e: e.tensor_add(QB[4][:, :F], QB[4][:, :F], QB[5][:, :F]),
                  [(AC, mc[3])])
        mc7 = act(lambda e: e.activation(QB[5][:, :F], t_a[:, :F], AF.Sign,
                                         bias=s_thr[:, 6:7]), aw + [(DV, m56)])
        mv1 = vec(lambda e: e.scalar_tensor_tensor(QB[2][:, :F], QB[2][:, :F], 0.5,
                                                   QB[0][:, :F], ALU.mult, ALU.add),
                  [(DV, m34), (DV, m12)])
        mv2 = vec(lambda e: e.scalar_tensor_tensor(QB[4][:, :F], QB[5][:, :F], 2.0,
                                                   QB[4][:, :F], ALU.mult, ALU.add),
                  [(AC, mc7), (DV, m56)])
        mq2 = vec(lambda e: e.tensor_add(QB[2][:, :F], QB[2][:, :F], QB[4][:, :F]),
                  [(DV, mv2), (DV, mv1)])
        scb = s_sc[:, :NB].rearrange("p (a o) -> p a o", o=1).broadcast_to([128, NB, 16])
        mt_ = vec(lambda e: e.scalar_tensor_tensor(QB[3][:, :F].rearrange("p (a b) -> p a b", b=16),
                                                   QB[2][:, :F].rearrange("p (a b) -> p a b", b=16),
                                                   5.0, scb, ALU.add, ALU.mult),
                  [(DV, mq2), (DV, msc)])
        msg = act(lambda e: e.activation(QB[1][:, :F], src, AF.Sign), w)
        mq = vec(lambda e: e.tensor_tensor(dst, QB[3][:, :F], QB[1][:, :F], ALU.mult),
                 [(DV, mt_), (AC, msg)])
        return (DV, mq)

    def layernorm(src, dst, src_ready):
        prog.wait("vector", AC, AC.n)
        prog.wait("scalar", DV, DV.n)
        w = list(src_ready)
        m1 = vec(lambda e: e.bn_stats(s_bn[:, 0:6], src[:, 0:512]), w)
        m1 = vec(lambda e: e.bn_stats(s_bn[:, 6:12], src[:, 512:1024]), w)
        m2 = vec(lambda e: e.bn_aggr(s_mv[:], s_bn[:]), [(DV, m1)])
        m3 = vec(lambda e: e.tensor_scalar(s_ve[:], s_mv[:, 1:2], EPS, None, ALU.add), [(DV, m2)])
        m4 = vec(lambda e: e.reciprocal(s_ve[:], s_ve[:]), [(DV, m3)])
        m5 = act(lambda e: e.activation(s_rs[:], s_ve[:], AF.Sqrt), [(DV, m4)])
        m6 = vec(lambda e: e.tensor_tensor(s_nm[:], s_mv[:, 0:1], s_rs[:], ALU.mult), [(AC, m5)])
        m7 = vec(lambda e: e.tensor_scalar(s_nm[:], s_nm[:], -1.0, None, ALU.mult), [(DV, m6)])
        m8 = act(lambda e: e.activation(dst, src, AF.Identity, bias=s_nm[:], scale=s_rs[:]),
                 [(DV, m7)])
        return (AC, m8)

    vec(lambda e: e.memset(s_one[:], 1.0))
    for i, mv in enumerate(MID):
        vec(lambda e, i=i, mv=mv: e.memset(s_thr[:, i:i + 1], -mv))

    def load_cast(dst_sb, dram, rows, cols):
        last = DV.n
        for kt in range(rows // 128):
            for cc in range(0, cols, 512):
                cl = min(512, cols - cc)
                ml = dma(s_wld[:, :cl], dram[kt * 128:(kt + 1) * 128, cc:cc + cl],
                         waits=[(DV, last)], sem=SWL)
                last = vec(lambda e, kt=kt, cc=cc, cl=cl: e.tensor_copy(
                    dst_sb[:, kt * cols + cc:kt * cols + cc + cl], s_wld[:, :cl]),
                    [(SWL, ml)])
        return last

    m_win = load_cast(swin, dwin, D, DS)
    m_wg = load_cast(swgo, dwgm, D, D)
    m_ct = load_cast(sct, dct, DS, D)
    m_dt = load_cast(sdt, ddt, DS, D)

    def load_quad(dst_sb, dram):
        last = DV.n
        for kt in range(2):
            ml = dma(s_wld[:, :DS], dram[kt * 128:(kt + 1) * 128, :],
                     waits=[(DV, last)], sem=SWL)
            for mt in range(2):
                last = vec(lambda e, kt=kt, mt=mt: e.tensor_copy(
                    dst_sb[:, (2 * kt + mt) * 128:(2 * kt + mt) * 128 + 128],
                    s_wld[:, mt * 128:(mt + 1) * 128]), [(SWL, ml)])
        return last

    m_A = load_quad(sA, dam)
    m_Bm = load_quad(sBm, dbm)
    m_bin = dma(s_bin[:], dbin)
    m_bg = dma(s_bias[:], dbgm)

    def prep_ffn(dram_w, dram_q, rows, cols):
        hist = [((DV, DV.n), (AC, AC.n)), ((DV, DV.n), (AC, AC.n))]
        st_mark = {0: 0, 1: 0}
        it = 0
        for rt in range(rows // 128):
            for cc in range(0, cols, 512):
                cl = min(512, cols - cc)
                par = it % 2
                wo = par * 512
                ml = dma(s_wld[:, wo:wo + cl], dram_w[rt * 128:(rt + 1) * 128, cc:cc + cl],
                         waits=list(hist[-2]), sem=SWL)
                mq = quantize(s_wld[:, wo:wo + cl], s_wq[:, wo:wo + cl], cl, [(SWL, ml)])
                mtp = dmat(s_tp[:, wo:wo + cl].rearrange("p (k c) -> p k c", c=128),
                           s_wq[:, wo:wo + cl].rearrange("p (k c) -> p k c", c=128),
                           [mq, (ST, st_mark[par])], sem=TP[par])
                for j in range(cl // 128):
                    dma(dram_q[cc // 128 + j, :, rt * 128:(rt + 1) * 128],
                        s_tp[:, wo + j * 128:wo + (j + 1) * 128],
                        waits=[(TP[par], mtp)], sem=ST)
                st_mark[par] = ST.n
                hist.append((mq, (AC, AC.n)))
                it += 1
        return hist[-1][0]

    prep_ffn(dwg, dwgq, I, D)
    prep_ffn(dwu, dwuq, I, D)
    prep_ffn(dwd, dwdq, D, I)

    def ffn_phase(dsrc, ddst, dbg=False):
        for c in range(NCH):
            r0 = c * CH
            join_all()
            mlx = dma(sxa[:].rearrange("p (m d) -> p m d", d=D),
                      dsrc[r0:r0 + CH, :].rearrange("(m p) d -> p m d", p=128))
            mqTs = []
            for mt in range(NMT):
                mln = layernorm(sxa[:, mt * D:(mt + 1) * D], sxb_[:, mt * D:(mt + 1) * D],
                                [(LD, mlx)])
                mq = quantize(sxb_[:, mt * D:(mt + 1) * D], sq_[:, mt * D:(mt + 1) * D],
                              D, [mln])
                dstv = sqT[:, mt * D:(mt + 1) * D].rearrange("p (k c) -> p k c", c=128)
                srcv = sq_[:, mt * D:(mt + 1) * D].rearrange("p (k c) -> p k c", c=128)
                mqTs.append((TQ[mt], dmat(dstv, srcv, [mq], sem=TQ[mt])))
                if dbg and dxq is not None:
                    dma(dxq[r0 + mt * 128:r0 + (mt + 1) * 128, :],
                        sq_[:, mt * D:(mt + 1) * D], waits=[mq], sem=ST)
            pe_after_slice = {}
            for si, (soff, slen) in enumerate(SLICES):
                par = si % 2
                prog.wait("sync", PE, pe_after_slice.get(si - 2, 0))
                for kt in range(KD):
                    dma(swg[par][:, kt * NS:kt * NS + slen], dwgq[kt, :, soff:soff + slen],
                        sem=TW[par])
                mwu = None
                for kt in range(KD):
                    mwu = dma(swu[par][:, kt * NS:kt * NS + slen], dwuq[kt, :, soff:soff + slen],
                              sem=TW[par])
                for mt in range(NMT):
                    for kt in range(KD):
                        pe(lambda e, kt=kt, mt=mt, par=par, slen=slen: e.matmul(
                            pp[0][:, :slen],
                            sqT[:, mt * D + kt * 128:mt * D + kt * 128 + 128],
                            swg[par][:, kt * NS:kt * NS + slen],
                            start=(kt == 0), stop=(kt == KD - 1)),
                           waits=([mqTs[mt], (TW[par], mwu)] if kt == 0 else ()),
                           inc=(kt == KD - 1))
                    mpg = PE.n
                    for kt in range(KD):
                        pe(lambda e, kt=kt, mt=mt, par=par, slen=slen: e.matmul(
                            pp[1][:, :slen],
                            sqT[:, mt * D + kt * 128:mt * D + kt * 128 + 128],
                            swu[par][:, kt * NS:kt * NS + slen],
                            start=(kt == 0), stop=(kt == KD - 1)),
                           inc=(kt == KD - 1))
                    mpu = PE.n
                    msil = act(lambda e, slen=slen: e.activation(t_sil[:, :slen], pp[0][:, :slen],
                                                                 AF.Silu), [(PE, mpg)])
                    mh = vec(lambda e, slen=slen: e.tensor_tensor(t_sil[:, :slen], t_sil[:, :slen],
                                                                  pp[1][:, :slen], ALU.mult),
                             [(AC, msil), (PE, mpu), (DV, DV.n)])
                    prog.wait("tensor", AC, msil)
                    prog.wait("tensor", DV, mh)
                    mhq = quantize(t_sil[:, :slen], t_hq[:, :slen], slen, [(DV, mh)])
                    dstv = s_hqT[:, mt * I + soff:mt * I + soff + slen].rearrange(
                        "p (k c) -> p k c", c=128)
                    srcv = t_hq[:, :slen].rearrange("p (k c) -> p k c", c=128)
                    dmat(dstv, srcv, [mhq], sem=TH[mt])
                    if dbg and dhq is not None:
                        dma(dhq[r0 + mt * 128:r0 + (mt + 1) * 128, soff:soff + slen],
                            t_hq[:, :slen], waits=[mhq], sem=ST)
                pe_after_slice[si] = PE.n
            th_marks = [(TH[mt], TH[mt].n) for mt in range(NMT)]
            for half in range(2):
                mwd_prev = {-2: PE.n, -1: PE.n}
                for ktg in range(KI):
                    par = ktg % 2
                    prog.wait("sync", PE, mwd_prev[ktg - 2])
                    mwd = dma(swd[par][:], dwdq[ktg, :, half * 512:(half + 1) * 512],
                              sem=TD[par])
                    for mt in range(NMT):
                        pe(lambda e, ktg=ktg, mt=mt, par=par: e.matmul(
                            pp[2 + mt][:, :],
                            s_hqT[:, mt * I + ktg * 128:mt * I + ktg * 128 + 128],
                            swd[par][:],
                            start=(ktg == 0), stop=(ktg == KI - 1)),
                           waits=(th_marks + [(TD[par], mwd)] if mt == 0 else ()),
                           inc=(mt == NMT - 1))
                    mwd_prev[ktg] = PE.n
                mdn = PE.n
                for mt in range(NMT):
                    o = mt * D + half * 512
                    mev = vec(lambda e, mt=mt, o=o: e.scalar_tensor_tensor(
                        sxa[:, o:o + 512], pp[2 + mt][:, :], 0.5, sxa[:, o:o + 512],
                        ALU.mult, ALU.add), [(PE, mdn)])
                    prog.wait("tensor", DV, mev)
            mst = DV.n
            dma(ddst[r0:r0 + CH, :].rearrange("(m p) d -> p m d", p=128),
                sxa[:].rearrange("p (m d) -> p m d", d=D), waits=[(DV, mst)])

    ffn_phase(dx, dx1, dbg=debug)

    for c in range(NCH):
        r0 = c * CH
        join_all()
        mlx = dma(sxa[:].rearrange("p (m d) -> p m d", d=D),
                  dx1[r0:r0 + CH, :].rearrange("(m p) d -> p m d", p=128))
        for mt in range(NMT):
            mln = layernorm(sxa[:, mt * D:(mt + 1) * D], sxb_[:, mt * D:(mt + 1) * D],
                            [(LD, mlx)])
            mstx = dma(dxin[r0 + mt * 128:r0 + (mt + 1) * 128, :],
                       sxb_[:, mt * D:(mt + 1) * D], waits=[mln], sem=ST)
            mcb = vec(lambda e, mt=mt: e.tensor_copy(sq_[:, mt * D:(mt + 1) * D],
                                                     sxb_[:, mt * D:(mt + 1) * D]), [mln])
            dstv = sqT[:, mt * D:(mt + 1) * D].rearrange("p (k c) -> p k c", c=128)
            srcv = sq_[:, mt * D:(mt + 1) * D].rearrange("p (k c) -> p k c", c=128)
            mqT = dmat(dstv, srcv, [(DV, mcb)], sem=TQ[mt])
            if debug and dqt is not None:
                dma(dqt[r0 + mt * 128:r0 + (mt + 1) * 128, :], sqT[:, mt * D:(mt + 1) * D],
                    waits=[(TQ[mt], mqT)], sem=ST)
            for kt in range(KD):
                pe(lambda e, kt=kt, mt=mt: e.matmul(
                    pp[0][:, :DS],
                    sqT[:, mt * D + kt * 128:mt * D + kt * 128 + 128],
                    swin[:, kt * DS:(kt + 1) * DS], start=(kt == 0), stop=False),
                   waits=([(TQ[mt], mqT), (DV, m_win), (LD, m_bin)] if kt == 0 else ()),
                   inc=False)
            mu_ = pe(lambda e: e.matmul(pp[0][:, :DS], s_one[:, :128], s_bin[:],
                                        start=False, stop=True))
            dbgw = []
            if debug and dup is not None:
                mevd = act(lambda e: e.activation(t_sil[:, :DS], pp[0][:, :DS], AF.Identity),
                           [(PE, mu_)])
                dma(dup[r0 + mt * 128:r0 + (mt + 1) * 128, :], t_sil[:, :DS],
                    waits=[(AC, mevd)], sem=ST)
                prog.wait("tensor", AC, mevd)
                dbgw = [(AC, mevd)]
            mev = vec(lambda e, mt=mt: e.tensor_copy(su_[:, mt * DS:(mt + 1) * DS],
                                                     pp[0][:, :DS]), [(PE, mu_)] + dbgw)
            prog.wait("tensor", DV, mev)
            dma(du[r0 + mt * 128:r0 + (mt + 1) * 128, :], su_[:, mt * DS:(mt + 1) * DS],
                waits=[(DV, mev)], sem=ST)
            for half in range(2):
                for kt in range(KD):
                    pe(lambda e, kt=kt, mt=mt, half=half: e.matmul(
                        pp[2 + half][:, :],
                        sqT[:, mt * D + kt * 128:mt * D + kt * 128 + 128],
                        swgo[:, kt * D + half * 512:kt * D + half * 512 + 512],
                        start=(kt == 0), stop=False),
                       waits=([(DV, m_wg), (LD, m_bg)] if kt == 0 else ()), inc=False)
                pe(lambda e, half=half: e.matmul(pp[2 + half][:, :], s_one[:, :128],
                                                 s_bias[:, half * 512:(half + 1) * 512],
                                                 start=False, stop=True))
            mg_ = PE.n
            mgs = None
            for half in range(2):
                mgs = act(lambda e, half=half: e.activation(
                    t_a[:, half * 512:half * 512 + 512],
                    pp[2 + half][:, :], AF.Sigmoid),
                    [(PE, mg_), (DV, DV.n)])
            prog.wait("tensor", AC, mgs)
            mgst = dma(dg[r0 + mt * 128:r0 + (mt + 1) * 128, :], t_a[:, :D],
                       waits=[(AC, mgs)], sem=SG)
            prog.wait("scalar", SG, mgst)

    def load_uT_window(w, buf):
        t0 = w * W
        prog.wait("sync", PE, PE.n)
        last = None
        for kt in range(KS):
            for b in range(BPC):
                for tb in range(W // 128):
                    srcv = du[b * T + t0 + tb * 128:b * T + t0 + (tb + 1) * 128,
                              kt * 128:(kt + 1) * 128]
                    dstc = kt * (BPC * W) + b * W + tb * 128
                    last = dmat(suTw[buf][:, dstc:dstc + 128], srcv, sem=TU[buf])
        return (TU[buf], last)

    def vfill(pA, pB, muT, buf, extra=()):
        for pk, parity in ((pA, 0), (pB, 1)):
            for mt in range(2):
                for kt in range(KS):
                    view = suTw[buf][:, kt * (BPC * W):(kt + 1) * (BPC * W)].rearrange(
                        "p (b te two) -> p te b two", b=BPC, two=2)
                    rhs = view[:, :, :, parity]
                    last = (parity == 1 and mt == 1 and kt == KS - 1)
                    pe(lambda e, pk=pk, mt=mt, kt=kt, rhs=rhs: e.matmul(
                        pk[:, mt * 256:(mt + 1) * 256],
                        sBm[:, (2 * kt + mt) * 128:(2 * kt + mt) * 128 + 128],
                        rhs, start=(mt == 0 and kt == 0), stop=(mt == 1 and kt == KS - 1)),
                       waits=(([muT, (DV, m_Bm)] + list(extra))
                              if (mt == 0 and kt == 0 and parity == 0) else ()),
                       inc=last)

    join_all(full=True)
    m_uT = load_uT_window(0, 0)
    vfill(pp[4], pp[5], m_uT, 0)
    m_vf0 = PE.n
    if debug and dvb is not None:
        mdbg = []
        for bi, pk in enumerate((pp[4], pp[5])):
            mdbg.append(act(lambda e, pk=pk, bi=bi: e.activation(
                sxa[:, bi * 512:(bi + 1) * 512], pk[:, :], AF.Identity), [(PE, m_vf0)]))
        dma(dvb[:, 0:512], sxa[:, 0:512], waits=[(AC, mdbg[0])], sem=ST)
        dma(dvb[:, 512:1024], sxa[:, 512:1024], waits=[(AC, mdbg[1])], sem=ST)
        prog.wait("tensor", AC, mdbg[-1])
        for s in ALL_DMA:
            prog.wait("tensor", s, s.n) if False else None
        prog.wait("tensor", ST, ST.n)

    for w in range(NW):
        cur = (pp[4], pp[5]) if w % 2 == 0 else (pp[6], pp[7])
        nxt = (pp[6], pp[7]) if w % 2 == 0 else (pp[4], pp[5])
        m_uTn = load_uT_window(w + 1, (w + 1) % 2) if w + 1 < NW else None
        for tl in range(W):
            t = w * W + tl
            bank = cur[tl % 2]
            c0 = tl // 2
            if t == 0:
                mpe = m_vf0
            else:
                hsrc = 4 * (t - 1)
                for kt in range(KS):
                    for mt in range(2):
                        outv = bank[:, mt * 256 + 2 * c0:mt * 256 + 2 * c0 + 2]
                        pe(lambda e, outv=outv, kt=kt, mt=mt, hsrc=hsrc: e.matmul(
                            outv,
                            sA[:, (2 * kt + mt) * 128:(2 * kt + mt) * 128 + 128],
                            sHT[:, hsrc + 2 * kt:hsrc + 2 * kt + 2],
                            start=False, stop=(kt == KS - 1)),
                           waits=([(DV, m_A)] if t == 1 and kt == 0 and mt == 0 else ()),
                           inc=(kt == KS - 1 and mt == 1))
                mpe = PE.n
            srcv = bank[:].rearrange("p (m r) -> p m r", m=2)[:, :, 2 * c0:2 * c0 + 2]
            dstv = sHT[:, 4 * t:4 * t + 4].rearrange("p (k b) -> p k b", k=2)
            ma = _op("scalar", ACS,
                     lambda e, s=srcv, d=dstv: e.activation(d, s, AF.Tanh),
                     [(PE, mpe)])
            prog.wait("tensor", ACS, ma)
            if tl == 16 and w + 1 < NW:
                vfill(nxt[0], nxt[1], m_uTn, (w + 1) % 2)

    prog.wait("sync", PE, PE.n)
    m_wo = load_cast(swgo, dwo, D, D)
    m_bo = dma(s_bias[:], dbo)
    for c in range(NCH):
        r0 = c * CH
        b = r0 // T
        join_all()
        mlxin = dma(sxb_[:].rearrange("p (m d) -> p m d", d=D),
                    dxin[r0:r0 + CH, :].rearrange("(m p) d -> p m d", p=128))
        mlx1 = dma(sxa[:].rearrange("p (m d) -> p m d", d=D),
                   dx1[r0:r0 + CH, :].rearrange("(m p) d -> p m d", p=128), sem=LX1)
        muT = None
        for kt in range(KS):
            for tb in range(NMT):
                srcv = du[r0 + tb * 128:r0 + (tb + 1) * 128, kt * 128:(kt + 1) * 128]
                muT = dmat(suT[:, kt * CH + tb * 128:kt * CH + (tb + 1) * 128], srcv,
                           sem=TU[0])
        muT = (TU[0], muT)
        for mt in range(NMT):
            grow = r0 + mt * 128
            t0 = grow % T
            mg_ld = dma(t_a[:, :D], dg[grow:grow + 128, :], waits=[(DV, DV.n)], sem=LG)
            for half in range(2):
                pk = pp[2 + half]
                for kt in range(KS):
                    lhs = sHT[:].rearrange("p (t f) -> p t f", f=4)[:, t0:t0 + 128, 2 * kt + b]
                    pe(lambda e, pk=pk, lhs=lhs, kt=kt, half=half: e.matmul(
                        pk[:, :], lhs,
                        sct[:, kt * D + half * 512:kt * D + half * 512 + 512],
                        start=(kt == 0), stop=False),
                       waits=([(ACS, ACS.n), (DV, m_ct)] if (kt == 0 and half == 0) else ()),
                       inc=False)
                for kt in range(KS):
                    pe(lambda e, pk=pk, kt=kt, mt=mt, half=half: e.matmul(
                        pk[:, :],
                        suT[:, kt * CH + mt * 128:kt * CH + (mt + 1) * 128],
                        sdt[:, kt * D + half * 512:kt * D + half * 512 + 512],
                        start=False, stop=(kt == KS - 1)),
                       waits=([muT, (DV, m_dt)] if (kt == 0 and half == 0) else ()),
                       inc=(kt == KS - 1))
            mY = PE.n
            mt3 = None
            for half in range(2):
                pk = pp[2 + half]
                o = mt * D + half * 512
                mt1 = vec(lambda e, pk=pk, o=o: e.scalar_tensor_tensor(
                    t_sil[:, :512], pk[:, :], 1.0, sxb_[:, o:o + 512],
                    ALU.mult, ALU.subtract),
                    [(PE, mY), (LD, mlxin), (LG, mg_ld), (AC, AC.n)])
                mt2 = vec(lambda e, o=o, half=half: e.tensor_tensor(
                    t_sil[:, :512], t_sil[:, :512], t_a[:, half * 512:half * 512 + 512],
                    ALU.mult), [(DV, mt1)])
                mt3 = vec(lambda e, o=o: e.tensor_tensor(sq_[:, o:o + 512], t_sil[:, :512],
                                                         sxb_[:, o:o + 512], ALU.add),
                          [(DV, mt2)])
                prog.wait("tensor", DV, mt3)
            dstv = sqT[:, mt * D:(mt + 1) * D].rearrange("p (k c) -> p k c", c=128)
            srcv = sq_[:, mt * D:(mt + 1) * D].rearrange("p (k c) -> p k c", c=128)
            mygT = dmat(dstv, srcv, [(DV, mt3)], sem=TQ[mt])
            for half in range(2):
                pk = pp[4 + half]
                for kt in range(KD):
                    pe(lambda e, pk=pk, kt=kt, mt=mt, half=half: e.matmul(
                        pk[:, :],
                        sqT[:, mt * D + kt * 128:mt * D + kt * 128 + 128],
                        swgo[:, kt * D + half * 512:kt * D + half * 512 + 512],
                        start=(kt == 0), stop=False),
                       waits=([(TQ[mt], mygT), (DV, m_wo), (LD, m_bo)] if kt == 0 else ()),
                       inc=False)
                pe(lambda e, pk=pk, half=half: e.matmul(
                    pk[:, :], s_one[:, :128], s_bias[:, half * 512:(half + 1) * 512],
                    start=False, stop=True))
            mW = PE.n
            me2 = None
            for half in range(2):
                pk = pp[4 + half]
                o = mt * D + half * 512
                me1 = vec(lambda e, pk=pk, o=o: e.scalar_tensor_tensor(
                    t_sil[:, :512], pk[:, :], 1.0, sxa[:, o:o + 512], ALU.mult, ALU.add),
                    [(PE, mW), (LX1, mlx1)])
                me2 = vec(lambda e, o=o: e.tensor_tensor(sxa[:, o:o + 512], t_sil[:, :512],
                                                         sxb_[:, o:o + 512], ALU.add),
                          [(DV, me1)])
                prog.wait("tensor", DV, me2)
        mx2 = DV.n
        dma(dx2[r0:r0 + CH, :].rearrange("(m p) d -> p m d", p=128),
            sxa[:].rearrange("p (m d) -> p m d", d=D), waits=[(DV, mx2)], sem=ST)

    ffn_phase(dx2, dy)

    if debug and dwdbg is not None:
        dma(dwdbg, swin[:], sem=ST)
        dma(dht, sHT[:], sem=ST)
    join_all(full=True)

    with nc.Block() as block:
        @block.sync
        def _(sync):
            for f in prog.ops["sync"]:
                f(sync)

        @block.vector
        def _(vector):
            for f in prog.ops["vector"]:
                f(vector)

        @block.scalar
        def _(scalar):
            for f in prog.ops["scalar"]:
                f(scalar)

        @block.tensor
        def _(tensor):
            for f in prog.ops["tensor"]:
                f(tensor)

        @block.gpsimd
        def _(gpsimd):
            for f in prog.ops["gpsimd"]:
                f(gpsimd)

    es.close()
    return nc


_NC_CACHE = {}


def _prep_inputs(L0_attn_x, L0_attn_W_in, L0_attn_b_in, L0_attn_W_gate, L0_attn_b_gate,
                 L0_attn_A, L0_attn_Bm, L0_attn_C, L0_attn_D, L0_attn_W_out, L0_attn_b_out,
                 L0_ffn_w_gate, L0_ffn_w_up, L0_ffn_w_down):
    x = np.ascontiguousarray(np.asarray(L0_attn_x, dtype=np.float32))
    B, T = x.shape[0], x.shape[1]
    shards = [np.ascontiguousarray(x[BPC * c:BPC * (c + 1)].reshape(BPC * T, D))
              for c in range(NCORES)]
    com = {
        "w_in": np.ascontiguousarray(np.asarray(L0_attn_W_in, np.float32)),
        "b_in": np.ascontiguousarray(np.asarray(L0_attn_b_in, np.float32).reshape(1, DS)),
        "w_gate_m": np.ascontiguousarray(np.asarray(L0_attn_W_gate, np.float32)),
        "b_gate_m": np.ascontiguousarray(np.asarray(L0_attn_b_gate, np.float32).reshape(1, D)),
        "a_m": np.ascontiguousarray(np.asarray(L0_attn_A, np.float32)),
        "bm_m": np.ascontiguousarray(np.asarray(L0_attn_Bm, np.float32)),
        "c_t": np.ascontiguousarray(np.asarray(L0_attn_C, np.float32).T),
        "d_t": np.ascontiguousarray(np.asarray(L0_attn_D, np.float32).T),
        "w_out": np.ascontiguousarray(np.asarray(L0_attn_W_out, np.float32)),
        "b_out": np.ascontiguousarray(np.asarray(L0_attn_b_out, np.float32).reshape(1, D)),
        "ffn_wg": np.ascontiguousarray(np.asarray(L0_ffn_w_gate, np.float32)),
        "ffn_wu": np.ascontiguousarray(np.asarray(L0_ffn_w_up, np.float32)),
        "ffn_wd": np.ascontiguousarray(np.asarray(L0_ffn_w_down, np.float32)),
    }
    return shards, com, B, T


def kernel(**inputs):
    shards, com, B, T = _prep_inputs(**inputs)
    if T not in _NC_CACHE:
        _NC_CACHE[T] = build_nc(T)
    nc = _NC_CACHE[T]
    in_maps = [dict(com, x=shards[c]) for c in range(NCORES)]
    res = run_bass_kernel_spmd(nc, in_maps, core_ids=list(range(NCORES)))
    out = np.empty((B, T, D), np.float32)
    for c in range(NCORES):
        out[BPC * c:BPC * (c + 1)] = res.results[c]["y"].reshape(BPC, T, D)
    return out


# version 08
# speedup vs baseline: 1.0566x; 1.0566x over previous
import contextlib
import numpy as np
import concourse.bass as bass
import concourse.mybir as mybir
from concourse.bass_utils import run_bass_kernel_spmd

F32 = mybir.dt.float32
BF16 = mybir.dt.bfloat16
AF = mybir.ActivationFunctionType
ALU = mybir.AluOpType
AX = mybir.AxisListType

NCORES = 8
B_GLOBAL = 16
D = 1024
DS = 256
I = 2816
BPC = B_GLOBAL // NCORES
MID = (0.25, 0.75, 1.25, 1.75, 2.5, 3.5, 5.0)
EPS = 1e-5
KD = D // 128
KS = DS // 128
KI = I // 128

ENGINES = ("sync", "vector", "scalar", "tensor", "gpsimd")


class Sem:
    def __init__(self, nc, es, name):
        self.h = es.enter_context(nc.semaphore(name))
        self.name = name
        self.n = 0


class Prog:
    def __init__(self):
        self.ops = {e: [] for e in ENGINES}
        self.waited = {e: {} for e in ENGINES}

    def emit(self, eng, fn):
        self.ops[eng].append(fn)

    def wait(self, eng, sem, val):
        if val is None or val <= 0:
            return
        if self.waited[eng].get(sem.name, 0) >= val:
            return
        self.waited[eng][sem.name] = val
        self.ops[eng].append(lambda e, s=sem.h, v=val: e.wait_ge(s, v))


def build_nc(T=4096, debug=False):
    TOK = BPC * T
    CH = 512
    NMT = CH // 128
    NCH = TOK // CH
    NS = 512
    SLICES = [(s, min(NS, I - s)) for s in range(0, I, NS)]
    W = 256
    NW = T // W

    nc = bass.Bass("TRN2", target_bir_lowering=False, debug=False)
    es = contextlib.ExitStack()

    dx = nc.dram_tensor("x", [TOK, D], F32, kind="ExternalInput").ap()
    dwin = nc.dram_tensor("w_in", [D, DS], F32, kind="ExternalInput").ap()
    dbin = nc.dram_tensor("b_in", [1, DS], F32, kind="ExternalInput").ap()
    dwgm = nc.dram_tensor("w_gate_m", [D, D], F32, kind="ExternalInput").ap()
    dbgm = nc.dram_tensor("b_gate_m", [1, D], F32, kind="ExternalInput").ap()
    dam = nc.dram_tensor("a_m", [DS, DS], F32, kind="ExternalInput").ap()
    dbm = nc.dram_tensor("bm_m", [DS, DS], F32, kind="ExternalInput").ap()
    dct = nc.dram_tensor("c_t", [DS, D], F32, kind="ExternalInput").ap()
    ddt = nc.dram_tensor("d_t", [DS, D], F32, kind="ExternalInput").ap()
    dwo = nc.dram_tensor("w_out", [D, D], F32, kind="ExternalInput").ap()
    dbo = nc.dram_tensor("b_out", [1, D], F32, kind="ExternalInput").ap()
    dwg = nc.dram_tensor("ffn_wg", [I, D], F32, kind="ExternalInput").ap()
    dwu = nc.dram_tensor("ffn_wu", [I, D], F32, kind="ExternalInput").ap()
    dwd = nc.dram_tensor("ffn_wd", [D, I], F32, kind="ExternalInput").ap()
    dy = nc.dram_tensor("y", [TOK, D], F32, kind="ExternalOutput").ap()
    _ik = "ExternalOutput" if debug else "Internal"
    dx1 = nc.dram_tensor("x1s", [TOK, D], F32, kind=_ik).ap()
    dx2 = nc.dram_tensor("x2s", [TOK, D], F32, kind=_ik).ap()
    dxin = nc.dram_tensor("xins", [TOK, D], F32, kind=_ik).ap()
    dg = nc.dram_tensor("gs", [TOK, D], F32, kind=_ik).ap()
    du = nc.dram_tensor("us", [TOK, DS], BF16, kind=_ik).ap()
    dwgq = nc.dram_tensor("wgqT", [KD, 128, I], BF16, kind=_ik).ap()
    dwuq = nc.dram_tensor("wuqT", [KD, 128, I], BF16, kind=_ik).ap()
    dwdq = nc.dram_tensor("wdqT", [KI, 128, D], BF16, kind=_ik).ap()
    dxq = nc.dram_tensor("xqdbg", [TOK, D], BF16, kind=_ik).ap() if debug else None
    dhq = nc.dram_tensor("hqdbg", [TOK, I], BF16, kind=_ik).ap() if debug else None
    dqt = nc.dram_tensor("qtdbg", [TOK, D], BF16, kind=_ik).ap() if debug else None
    dup = nc.dram_tensor("updbg", [TOK, DS], F32, kind=_ik).ap() if debug else None
    dwdbg = nc.dram_tensor("wdbg", [128, KD * DS], BF16, kind=_ik).ap() if debug else None
    dht = nc.dram_tensor("htdbg", [128, 4 * T], BF16, kind=_ik).ap() if debug else None
    dvb = nc.dram_tensor("vbdbg", [128, 1024], F32, kind=_ik).ap() if debug else None

    def sb(name, shape, dt=BF16):
        return es.enter_context(nc.sbuf_tensor(name, shape, dt))

    swin = sb("swin", [128, KD * DS])
    swgo = sb("swgo", [128, KD * D])
    sct = sb("sct", [128, KS * D])
    sdt = sb("sdt", [128, KS * D])
    sA = sb("sA", [128, 4 * 128])
    sBm = sb("sBm", [128, 4 * 128])
    sHT = sb("sHT", [128, 4 * T])
    swg = [sb(f"swg{i}", [128, KD * NS]) for i in range(2)]
    swu = [sb(f"swu{i}", [128, KD * NS]) for i in range(2)]
    swd = [sb(f"swd{i}", [128, NS]) for i in range(2)]
    sxa = sb("sxa", [128, NMT * D], F32)
    sxb_ = sb("sxb", [128, NMT * D], F32)
    sq_ = sb("sq", [128, NMT * D])
    sqT = sb("sqT", [128, KD * CH])
    su_ = sb("su", [128, NMT * DS])
    suT = sb("suT", [128, KS * CH])
    suTw = [sb("suTw0", [128, KS * 2 * W])]
    s_hqT = sb("s_hqT", [128, KI * CH])
    t_sil = [sb(f"t_sil{i}", [128, NS], F32) for i in range(2)]
    t_hq = [sb(f"t_hq{i}", [128, NS]) for i in range(2)]
    t_a = [sb(f"t_a{i}", [128, NS], F32) for i in range(2)]
    t_g = sb("t_g", [128, D], F32)
    QB = [[sb(f"qb{p}_{i}", [128, NS]) for i in range(6)] for p in range(2)]
    s_bm = [sb(f"s_bm{i}", [128, NS // 16], F32) for i in range(2)]
    s_ri = [sb(f"s_ri{i}", [128, NS // 16], F32) for i in range(2)]
    s_sc = [sb(f"s_sc{i}", [128, NS // 16], F32) for i in range(2)]
    s_bn = sb("s_bn", [128, 12], F32)
    s_mv = sb("s_mv", [128, 2], F32)
    s_rs = sb("s_rs", [128, 1], F32)
    s_nm = sb("s_nm", [128, 1], F32)
    s_ve = sb("s_ve", [128, 1], F32)
    s_one = sb("s_one", [1, 128])
    s_bias = sb("s_bias", [1, D])
    s_bin = sb("s_bin", [1, DS])
    s_thr = sb("s_thr", [128, 8], F32)
    s_wld = sb("s_wld", [128, 1024], F32)
    s_tp = sb("s_tp", [128, 1024])

    pp = [es.enter_context(nc.psum_tensor(f"pp{i}", [128, 512], F32)) for i in range(8)]

    LD = Sem(nc, es, "LD")
    ST = Sem(nc, es, "ST")
    TW = [Sem(nc, es, f"TW{i}") for i in range(2)]
    TD = [Sem(nc, es, f"TD{i}") for i in range(2)]
    TQ = [Sem(nc, es, f"TQ{i}") for i in range(4)]
    TH = [Sem(nc, es, f"TH{i}") for i in range(4)]
    TU = [Sem(nc, es, f"TU{i}") for i in range(2)]
    TP = [Sem(nc, es, f"TP{i}") for i in range(2)]
    PE = Sem(nc, es, "PE")
    DV = Sem(nc, es, "DV")
    AC = Sem(nc, es, "AC")
    ACS = Sem(nc, es, "ACS")
    SG = Sem(nc, es, "SG")
    LX1 = Sem(nc, es, "LX1")
    LG = Sem(nc, es, "LG")
    SWL = Sem(nc, es, "SWL")
    ALL_DMA = [LD, ST, SG, LX1, LG, SWL] + TW + TD + TQ + TH + TU + TP

    prog = Prog()

    def dma(out, in_, waits=(), sem=None):
        sem = sem or LD
        for (s, v) in waits:
            prog.wait("sync", s, v)
        sem.n += 16
        prog.emit("sync", lambda e, o=out, i=in_, s=sem.h: e.dma_start(out=o, in_=i).then_inc(s, 16))
        return sem.n

    def dmat(out, in_, waits=(), sem=None):
        sem = sem or LD
        for (s, v) in waits:
            prog.wait("sync", s, v)
        sem.n += 16
        prog.emit("sync", lambda e, o=out, i=in_, s=sem.h: e.dma_start_transpose(o, i).then_inc(s, 16))
        return sem.n

    def join_all(full=False):
        sems = ALL_DMA if full else [LD, ST, SG, LX1, LG]
        for s in sems:
            prog.wait("sync", s, s.n)
        prog.wait("sync", PE, PE.n)
        prog.wait("sync", DV, DV.n)
        prog.wait("sync", AC, AC.n)
        prog.wait("sync", ACS, ACS.n)

    def _op(eng, sem, fn, waits):
        for (s, v) in waits:
            prog.wait(eng, s, v)
        sem.n += 1
        prog.emit(eng, lambda e, fn=fn, s=sem.h: fn(e).then_inc(s, 1))
        return sem.n

    def vec(fn, waits=()):
        return _op("vector", DV, fn, waits)

    def act(fn, waits=()):
        return _op("scalar", AC, fn, waits)

    def pe(fn, waits=(), inc=True, sem=None):
        for (s, v) in waits:
            prog.wait("tensor", s, v)
        if inc:
            s = sem or PE
            s.n += 1
            prog.emit("tensor", lambda e, fn=fn, sh=s.h: fn(e).then_inc(sh, 1))
            return s.n
        prog.emit("tensor", fn)
        return None

    qset_marks = [None, None]
    _qctr = [0]

    def quantize(src, dst, F, src_ready):
        p = _qctr[0] % 2
        _qctr[0] += 1
        NB = F // 16
        B = QB[p]
        ta, bm, ri, sc = t_a[p], s_bm[p], s_ri[p], s_sc[p]
        if qset_marks[p] is not None:
            pdv, pac = qset_marks[p]
            prog.wait("vector", AC, pac)
            prog.wait("scalar", DV, pdv)
        w = list(src_ready)
        mbm = vec(lambda e: e.tensor_reduce(bm[:, :NB],
                                            src.rearrange("p (a b) -> p a b", b=16),
                                            AX.X, ALU.max, apply_absolute_value=True), w)
        mbm = vec(lambda e: e.tensor_scalar(bm[:, :NB], bm[:, :NB], 6e-12, None, ALU.max),
                  [(DV, mbm)])
        mri = vec(lambda e: e.reciprocal(ri[:, :NB], bm[:, :NB]), [(DV, mbm)])
        mri = vec(lambda e: e.tensor_scalar(ri[:, :NB], ri[:, :NB], 6.0, None, ALU.mult),
                  [(DV, mri)])
        msc = vec(lambda e: e.tensor_scalar(sc[:, :NB], bm[:, :NB], 1.0 / 12.0, None,
                                            ALU.mult), [(DV, mbm)])
        mah = act(lambda e: e.activation(ta[:, :F], src, AF.Abs), w)
        rb = ri[:, :NB].rearrange("p (a o) -> p a o", o=1).broadcast_to([128, NB, 16])
        ma = vec(lambda e: e.tensor_tensor(ta[:, :F].rearrange("p (a b) -> p a b", b=16),
                                           ta[:, :F].rearrange("p (a b) -> p a b", b=16),
                                           rb, ALU.mult), [(DV, mri), (AC, mah)])
        aw = [(DV, ma)]
        m1 = vec(lambda e: e.tensor_scalar(B[0][:, :F], ta[:, :F], MID[0], None, ALU.is_gt), aw)
        m2 = vec(lambda e: e.tensor_scalar(B[1][:, :F], ta[:, :F], MID[1], None, ALU.is_gt), aw)
        mc = [act(lambda e, i=i, j=j: e.activation(B[j][:, :F], ta[:, :F], AF.Sign,
                                                   bias=s_thr[:, i:i + 1]), aw)
              for i, j in ((2, 2), (3, 3), (4, 4), (5, 5))]
        m12 = vec(lambda e: e.tensor_add(B[0][:, :F], B[0][:, :F], B[1][:, :F]), [(DV, m2)])
        m34 = vec(lambda e: e.tensor_add(B[2][:, :F], B[2][:, :F], B[3][:, :F]),
                  [(AC, mc[1])])
        m56 = vec(lambda e: e.tensor_add(B[4][:, :F], B[4][:, :F], B[5][:, :F]),
                  [(AC, mc[3])])
        mc7 = act(lambda e: e.activation(B[5][:, :F], ta[:, :F], AF.Sign,
                                         bias=s_thr[:, 6:7]), aw + [(DV, m56)])
        mv1 = vec(lambda e: e.scalar_tensor_tensor(B[2][:, :F], B[2][:, :F], 0.5,
                                                   B[0][:, :F], ALU.mult, ALU.add),
                  [(DV, m34), (DV, m12)])
        mv2 = vec(lambda e: e.scalar_tensor_tensor(B[4][:, :F], B[5][:, :F], 2.0,
                                                   B[4][:, :F], ALU.mult, ALU.add),
                  [(AC, mc7), (DV, m56)])
        mq2 = vec(lambda e: e.tensor_add(B[2][:, :F], B[2][:, :F], B[4][:, :F]),
                  [(DV, mv2), (DV, mv1)])
        scb = sc[:, :NB].rearrange("p (a o) -> p a o", o=1).broadcast_to([128, NB, 16])
        mt_ = vec(lambda e: e.scalar_tensor_tensor(B[3][:, :F].rearrange("p (a b) -> p a b", b=16),
                                                   B[2][:, :F].rearrange("p (a b) -> p a b", b=16),
                                                   5.0, scb, ALU.add, ALU.mult),
                  [(DV, mq2), (DV, msc)])
        msg = act(lambda e: e.activation(B[1][:, :F], src, AF.Sign), w + [(DV, m12)])
        mq = vec(lambda e: e.tensor_tensor(dst, B[3][:, :F], B[1][:, :F], ALU.mult),
                 [(DV, mt_), (AC, msg)])
        qset_marks[p] = (mq, max(msg, mc7))
        return (DV, mq)

    def layernorm(src, dst, src_ready):
        prog.wait("vector", AC, AC.n)
        prog.wait("scalar", DV, DV.n)
        w = list(src_ready)
        m1 = vec(lambda e: e.bn_stats(s_bn[:, 0:6], src[:, 0:512]), w)
        m1 = vec(lambda e: e.bn_stats(s_bn[:, 6:12], src[:, 512:1024]), w)
        m2 = vec(lambda e: e.bn_aggr(s_mv[:], s_bn[:]), [(DV, m1)])
        m3 = vec(lambda e: e.tensor_scalar(s_ve[:], s_mv[:, 1:2], EPS, None, ALU.add), [(DV, m2)])
        m4 = vec(lambda e: e.reciprocal(s_ve[:], s_ve[:]), [(DV, m3)])
        m5 = act(lambda e: e.activation(s_rs[:], s_ve[:], AF.Sqrt), [(DV, m4)])
        m6 = vec(lambda e: e.tensor_tensor(s_nm[:], s_mv[:, 0:1], s_rs[:], ALU.mult), [(AC, m5)])
        m7 = vec(lambda e: e.tensor_scalar(s_nm[:], s_nm[:], -1.0, None, ALU.mult), [(DV, m6)])
        m8 = act(lambda e: e.activation(dst, src, AF.Identity, bias=s_nm[:], scale=s_rs[:]),
                 [(DV, m7)])
        return (AC, m8)

    vec(lambda e: e.memset(s_one[:], 1.0))
    for i, mv in enumerate(MID):
        vec(lambda e, i=i, mv=mv: e.memset(s_thr[:, i:i + 1], -mv))

    def load_cast(dst_sb, dram, rows, cols):
        last = DV.n
        for kt in range(rows // 128):
            for cc in range(0, cols, 512):
                cl = min(512, cols - cc)
                ml = dma(s_wld[:, :cl], dram[kt * 128:(kt + 1) * 128, cc:cc + cl],
                         waits=[(DV, last)], sem=SWL)
                last = vec(lambda e, kt=kt, cc=cc, cl=cl: e.tensor_copy(
                    dst_sb[:, kt * cols + cc:kt * cols + cc + cl], s_wld[:, :cl]),
                    [(SWL, ml)])
        return last

    m_win = load_cast(swin, dwin, D, DS)
    m_wg = load_cast(swgo, dwgm, D, D)
    m_ct = load_cast(sct, dct, DS, D)
    m_dt = load_cast(sdt, ddt, DS, D)

    def load_quad(dst_sb, dram):
        last = DV.n
        for kt in range(2):
            ml = dma(s_wld[:, :DS], dram[kt * 128:(kt + 1) * 128, :],
                     waits=[(DV, last)], sem=SWL)
            for mt in range(2):
                last = vec(lambda e, kt=kt, mt=mt: e.tensor_copy(
                    dst_sb[:, (2 * kt + mt) * 128:(2 * kt + mt) * 128 + 128],
                    s_wld[:, mt * 128:(mt + 1) * 128]), [(SWL, ml)])
        return last

    m_A = load_quad(sA, dam)
    m_Bm = load_quad(sBm, dbm)
    m_binf = dma(s_wld[0:1, :DS], dbin, waits=[(DV, m_Bm)], sem=SWL)
    m_bin = vec(lambda e: e.tensor_copy(s_bin[:], s_wld[0:1, :DS]), [(SWL, m_binf)])
    m_bgf = dma(s_wld[0:1, :], dbgm, waits=[(DV, m_bin)], sem=SWL)
    m_bg = vec(lambda e: e.tensor_copy(s_bias[:], s_wld[0:1, :]), [(SWL, m_bgf)])

    def prep_ffn(dram_w, dram_q, rows, cols):
        hist = [((DV, DV.n), (AC, AC.n)), ((DV, DV.n), (AC, AC.n))]
        st_mark = {0: 0, 1: 0}
        tp_mark = {0: 0, 1: 0}
        it = 0
        for rt in range(rows // 128):
            for cc in range(0, cols, 512):
                cl = min(512, cols - cc)
                par = it % 2
                wo = par * 512
                ml = dma(s_wld[:, wo:wo + cl], dram_w[rt * 128:(rt + 1) * 128, cc:cc + cl],
                         waits=list(hist[-2]), sem=SWL)
                prog.wait("vector", TP[par], tp_mark[par])
                mq = quantize(s_wld[:, wo:wo + cl], t_hq[par][:, :cl], cl, [(SWL, ml)])
                mtp = dmat(s_tp[:, wo:wo + cl].rearrange("p (k c) -> p k c", c=128),
                           t_hq[par][:, :cl].rearrange("p (k c) -> p k c", c=128),
                           [mq, (ST, st_mark[par])], sem=TP[par])
                for j in range(cl // 128):
                    dma(dram_q[cc // 128 + j, :, rt * 128:(rt + 1) * 128],
                        s_tp[:, wo + j * 128:wo + (j + 1) * 128],
                        waits=[(TP[par], mtp)], sem=ST)
                st_mark[par] = ST.n
                tp_mark[par] = TP[par].n
                hist.append((mq, (AC, AC.n)))
                it += 1
        return hist[-1][0]

    prep_ffn(dwg, dwgq, I, D)
    prep_ffn(dwu, dwuq, I, D)
    prep_ffn(dwd, dwdq, D, I)

    def ffn_phase(dsrc, ddst, dbg=False):
        prog.wait("vector", TP[0], TP[0].n)
        prog.wait("vector", TP[1], TP[1].n)
        for c in range(NCH):
            r0 = c * CH
            join_all()
            mlx = dma(sxa[:].rearrange("p (m d) -> p m d", d=D),
                      dsrc[r0:r0 + CH, :].rearrange("(m p) d -> p m d", p=128))
            mqTs = []
            for mt in range(NMT):
                mln = layernorm(sxa[:, mt * D:(mt + 1) * D], sxb_[:, mt * D:(mt + 1) * D],
                                [(LD, mlx)])
                mqa = quantize(sxb_[:, mt * D:mt * D + 512], sq_[:, mt * D:mt * D + 512],
                               512, [mln])
                mqb = quantize(sxb_[:, mt * D + 512:mt * D + 1024],
                               sq_[:, mt * D + 512:mt * D + 1024], 512, [mln])
                dstv = sqT[:, mt * D:(mt + 1) * D].rearrange("p (k c) -> p k c", c=128)
                srcv = sq_[:, mt * D:(mt + 1) * D].rearrange("p (k c) -> p k c", c=128)
                mqTs.append((TQ[mt], dmat(dstv, srcv, [mqa, mqb], sem=TQ[mt])))
                if dbg and dxq is not None:
                    dma(dxq[r0 + mt * 128:r0 + (mt + 1) * 128, :],
                        sq_[:, mt * D:(mt + 1) * D], waits=[mqa, mqb], sem=ST)
            pe_after_slice = {}
            guards = {0: (0, 0), 1: (0, 0)}
            it = 0
            for si, (soff, slen) in enumerate(SLICES):
                par = si % 2
                prog.wait("sync", PE, pe_after_slice.get(si - 2, 0))
                for kt in range(KD):
                    dma(swg[par][:, kt * NS:kt * NS + slen], dwgq[kt, :, soff:soff + slen],
                        sem=TW[par])
                mwu = None
                for kt in range(KD):
                    mwu = dma(swu[par][:, kt * NS:kt * NS + slen], dwuq[kt, :, soff:soff + slen],
                              sem=TW[par])
                for mt in range(NMT):
                    bp = it % 2
                    pg = pp[0] if bp == 0 else pp[6]
                    pu = pp[1] if bp == 0 else pp[7]
                    prog.wait("tensor", AC, guards[bp][0])
                    prog.wait("tensor", DV, guards[bp][1])
                    for kt in range(KD):
                        pe(lambda e, kt=kt, mt=mt, par=par, slen=slen, pg=pg: e.matmul(
                            pg[:, :slen],
                            sqT[:, mt * D + kt * 128:mt * D + kt * 128 + 128],
                            swg[par][:, kt * NS:kt * NS + slen],
                            start=(kt == 0), stop=(kt == KD - 1)),
                           waits=([mqTs[mt], (TW[par], mwu)] if kt == 0 else ()),
                           inc=(kt == KD - 1))
                    mpg = PE.n
                    for kt in range(KD):
                        pe(lambda e, kt=kt, mt=mt, par=par, slen=slen, pu=pu: e.matmul(
                            pu[:, :slen],
                            sqT[:, mt * D + kt * 128:mt * D + kt * 128 + 128],
                            swu[par][:, kt * NS:kt * NS + slen],
                            start=(kt == 0), stop=(kt == KD - 1)),
                           inc=(kt == KD - 1))
                    mpu = PE.n
                    msil = act(lambda e, slen=slen, pg=pg, bp=bp: e.activation(
                        t_sil[bp][:, :slen], pg[:, :slen], AF.Silu),
                        [(PE, mpg), (DV, guards[bp][1])])
                    mh = vec(lambda e, slen=slen, pu=pu, bp=bp: e.tensor_tensor(
                        t_sil[bp][:, :slen], t_sil[bp][:, :slen], pu[:, :slen], ALU.mult),
                        [(AC, msil), (PE, mpu)])
                    mhq = quantize(t_sil[bp][:, :slen], t_hq[bp][:, :slen], slen, [(DV, mh)])
                    guards[bp] = (msil, mhq[1])
                    dstv = s_hqT[:, mt * I + soff:mt * I + soff + slen].rearrange(
                        "p (k c) -> p k c", c=128)
                    srcv = t_hq[bp][:, :slen].rearrange("p (k c) -> p k c", c=128)
                    dmat(dstv, srcv, [mhq], sem=TH[mt])
                    if dbg and dhq is not None:
                        dma(dhq[r0 + mt * 128:r0 + (mt + 1) * 128, soff:soff + slen],
                            t_hq[bp][:, :slen], waits=[mhq], sem=ST)
                    it += 1
                pe_after_slice[si] = PE.n
            th_marks = [(TH[mt], TH[mt].n) for mt in range(NMT)]
            for half in range(2):
                mwd_prev = {-2: PE.n, -1: PE.n}
                for ktg in range(KI):
                    par = ktg % 2
                    prog.wait("sync", PE, mwd_prev[ktg - 2])
                    mwd = dma(swd[par][:], dwdq[ktg, :, half * 512:(half + 1) * 512],
                              sem=TD[par])
                    for mt in range(NMT):
                        pe(lambda e, ktg=ktg, mt=mt, par=par: e.matmul(
                            pp[2 + mt][:, :],
                            s_hqT[:, mt * I + ktg * 128:mt * I + ktg * 128 + 128],
                            swd[par][:],
                            start=(ktg == 0), stop=(ktg == KI - 1)),
                           waits=(th_marks + [(TD[par], mwd)] if mt == 0 else ()),
                           inc=(mt == NMT - 1))
                    mwd_prev[ktg] = PE.n
                mdn = PE.n
                for mt in range(NMT):
                    o = mt * D + half * 512
                    mev = vec(lambda e, mt=mt, o=o: e.scalar_tensor_tensor(
                        sxa[:, o:o + 512], pp[2 + mt][:, :], 0.5, sxa[:, o:o + 512],
                        ALU.mult, ALU.add), [(PE, mdn)])
                    prog.wait("tensor", DV, mev)
            mst = DV.n
            dma(ddst[r0:r0 + CH, :].rearrange("(m p) d -> p m d", p=128),
                sxa[:].rearrange("p (m d) -> p m d", d=D), waits=[(DV, mst)])

    ffn_phase(dx, dx1, dbg=debug)

    for c in range(NCH):
        r0 = c * CH
        join_all()
        mlx = dma(sxa[:].rearrange("p (m d) -> p m d", d=D),
                  dx1[r0:r0 + CH, :].rearrange("(m p) d -> p m d", p=128))
        for mt in range(NMT):
            mln = layernorm(sxa[:, mt * D:(mt + 1) * D], sxb_[:, mt * D:(mt + 1) * D],
                            [(LD, mlx)])
            mstx = dma(dxin[r0 + mt * 128:r0 + (mt + 1) * 128, :],
                       sxb_[:, mt * D:(mt + 1) * D], waits=[mln], sem=ST)
            mcb = vec(lambda e, mt=mt: e.tensor_copy(sq_[:, mt * D:(mt + 1) * D],
                                                     sxb_[:, mt * D:(mt + 1) * D]), [mln])
            dstv = sqT[:, mt * D:(mt + 1) * D].rearrange("p (k c) -> p k c", c=128)
            srcv = sq_[:, mt * D:(mt + 1) * D].rearrange("p (k c) -> p k c", c=128)
            mqT = dmat(dstv, srcv, [(DV, mcb)], sem=TQ[mt])
            if debug and dqt is not None:
                dma(dqt[r0 + mt * 128:r0 + (mt + 1) * 128, :], sqT[:, mt * D:(mt + 1) * D],
                    waits=[(TQ[mt], mqT)], sem=ST)
            for kt in range(KD):
                pe(lambda e, kt=kt, mt=mt: e.matmul(
                    pp[0][:, :DS],
                    sqT[:, mt * D + kt * 128:mt * D + kt * 128 + 128],
                    swin[:, kt * DS:(kt + 1) * DS], start=(kt == 0), stop=False),
                   waits=([(TQ[mt], mqT), (DV, m_win), (DV, m_bin)] if kt == 0 else ()),
                   inc=False)
            mu_ = pe(lambda e: e.matmul(pp[0][:, :DS], s_one[:, :128], s_bin[:],
                                        start=False, stop=True))
            dbgw = []
            if debug and dup is not None:
                mevd = act(lambda e: e.activation(t_sil[0][:, :DS], pp[0][:, :DS], AF.Identity),
                           [(PE, mu_)])
                dma(dup[r0 + mt * 128:r0 + (mt + 1) * 128, :], t_sil[0][:, :DS],
                    waits=[(AC, mevd)], sem=ST)
                prog.wait("tensor", AC, mevd)
                dbgw = [(AC, mevd)]
            mev = vec(lambda e, mt=mt: e.tensor_copy(su_[:, mt * DS:(mt + 1) * DS],
                                                     pp[0][:, :DS]), [(PE, mu_)] + dbgw)
            prog.wait("tensor", DV, mev)
            dma(du[r0 + mt * 128:r0 + (mt + 1) * 128, :], su_[:, mt * DS:(mt + 1) * DS],
                waits=[(DV, mev)], sem=ST)
            for half in range(2):
                for kt in range(KD):
                    pe(lambda e, kt=kt, mt=mt, half=half: e.matmul(
                        pp[2 + half][:, :],
                        sqT[:, mt * D + kt * 128:mt * D + kt * 128 + 128],
                        swgo[:, kt * D + half * 512:kt * D + half * 512 + 512],
                        start=(kt == 0), stop=False),
                       waits=([(DV, m_wg), (DV, m_bg)] if kt == 0 else ()), inc=False)
                pe(lambda e, half=half: e.matmul(pp[2 + half][:, :], s_one[:, :128],
                                                 s_bias[:, half * 512:(half + 1) * 512],
                                                 start=False, stop=True))
            mg_ = PE.n
            mgs = None
            for half in range(2):
                mgs = act(lambda e, half=half: e.activation(
                    t_g[:, half * 512:half * 512 + 512],
                    pp[2 + half][:, :], AF.Sigmoid),
                    [(PE, mg_), (DV, DV.n)])
            prog.wait("tensor", AC, mgs)
            mgst = dma(dg[r0 + mt * 128:r0 + (mt + 1) * 128, :], t_g[:, :D],
                       waits=[(AC, mgs)], sem=SG)
            prog.wait("scalar", SG, mgst)

    def load_uT_window(w, buf):
        t0 = w * W
        prog.wait("sync", PE, PE.n)
        last = None
        for kt in range(KS):
            for b in range(BPC):
                for tb in range(W // 128):
                    srcv = du[b * T + t0 + tb * 128:b * T + t0 + (tb + 1) * 128,
                              kt * 128:(kt + 1) * 128]
                    dstc = kt * (BPC * W) + b * W + tb * 128
                    last = dmat(suTw[buf][:, dstc:dstc + 128], srcv, sem=TU[buf])
        return (TU[buf], last)

    def vfill(pA, pB, muT, buf, extra=()):
        for pk, parity in ((pA, 0), (pB, 1)):
            for mt in range(2):
                for kt in range(KS):
                    view = suTw[buf][:, kt * (BPC * W):(kt + 1) * (BPC * W)].rearrange(
                        "p (b te two) -> p te b two", b=BPC, two=2)
                    rhs = view[:, :, :, parity]
                    last = (parity == 1 and mt == 1 and kt == KS - 1)
                    pe(lambda e, pk=pk, mt=mt, kt=kt, rhs=rhs: e.matmul(
                        pk[:, mt * 256:(mt + 1) * 256],
                        sBm[:, (2 * kt + mt) * 128:(2 * kt + mt) * 128 + 128],
                        rhs, start=(mt == 0 and kt == 0), stop=(mt == 1 and kt == KS - 1)),
                       waits=(([muT, (DV, m_Bm)] + list(extra))
                              if (mt == 0 and kt == 0 and parity == 0) else ()),
                       inc=last)

    join_all(full=True)
    m_uT = load_uT_window(0, 0)
    vfill(pp[4], pp[5], m_uT, 0)
    m_vf0 = PE.n
    if debug and dvb is not None:
        mdbg = []
        for bi, pk in enumerate((pp[4], pp[5])):
            mdbg.append(act(lambda e, pk=pk, bi=bi: e.activation(
                sxa[:, bi * 512:(bi + 1) * 512], pk[:, :], AF.Identity), [(PE, m_vf0)]))
        dma(dvb[:, 0:512], sxa[:, 0:512], waits=[(AC, mdbg[0])], sem=ST)
        dma(dvb[:, 512:1024], sxa[:, 512:1024], waits=[(AC, mdbg[1])], sem=ST)
        prog.wait("tensor", AC, mdbg[-1])
        for s in ALL_DMA:
            prog.wait("tensor", s, s.n) if False else None
        prog.wait("tensor", ST, ST.n)

    for w in range(NW):
        cur = (pp[4], pp[5]) if w % 2 == 0 else (pp[6], pp[7])
        nxt = (pp[6], pp[7]) if w % 2 == 0 else (pp[4], pp[5])
        m_uTn = load_uT_window(w + 1, 0) if w + 1 < NW else None
        for tl in range(W):
            t = w * W + tl
            bank = cur[tl % 2]
            c0 = tl // 2
            if t == 0:
                mpe = m_vf0
            else:
                hsrc = 4 * (t - 1)
                for kt in range(KS):
                    for mt in range(2):
                        outv = bank[:, mt * 256 + 2 * c0:mt * 256 + 2 * c0 + 2]
                        pe(lambda e, outv=outv, kt=kt, mt=mt, hsrc=hsrc: e.matmul(
                            outv,
                            sA[:, (2 * kt + mt) * 128:(2 * kt + mt) * 128 + 128],
                            sHT[:, hsrc + 2 * kt:hsrc + 2 * kt + 2],
                            start=False, stop=(kt == KS - 1)),
                           waits=([(DV, m_A)] if t == 1 and kt == 0 and mt == 0 else ()),
                           inc=(kt == KS - 1 and mt == 1))
                mpe = PE.n
            srcv = bank[:].rearrange("p (m r) -> p m r", m=2)[:, :, 2 * c0:2 * c0 + 2]
            dstv = sHT[:, 4 * t:4 * t + 4].rearrange("p (k b) -> p k b", k=2)
            ma = _op("scalar", ACS,
                     lambda e, s=srcv, d=dstv: e.activation(d, s, AF.Tanh),
                     [(PE, mpe)])
            prog.wait("tensor", ACS, ma)
            if tl == 16 and w + 1 < NW:
                vfill(nxt[0], nxt[1], m_uTn, 0)

    prog.wait("sync", PE, PE.n)
    m_wo = load_cast(swgo, dwo, D, D)
    m_bof = dma(s_wld[0:1, :], dbo, waits=[(DV, DV.n)], sem=SWL)
    m_bo = vec(lambda e: e.tensor_copy(s_bias[:], s_wld[0:1, :]), [(SWL, m_bof)])
    for c in range(NCH):
        r0 = c * CH
        b = r0 // T
        join_all()
        mlxin = dma(sxb_[:].rearrange("p (m d) -> p m d", d=D),
                    dxin[r0:r0 + CH, :].rearrange("(m p) d -> p m d", p=128))
        mlx1 = dma(sxa[:].rearrange("p (m d) -> p m d", d=D),
                   dx1[r0:r0 + CH, :].rearrange("(m p) d -> p m d", p=128), sem=LX1)
        muT = None
        for kt in range(KS):
            for tb in range(NMT):
                srcv = du[r0 + tb * 128:r0 + (tb + 1) * 128, kt * 128:(kt + 1) * 128]
                muT = dmat(suT[:, kt * CH + tb * 128:kt * CH + (tb + 1) * 128], srcv,
                           sem=TU[0])
        muT = (TU[0], muT)
        for mt in range(NMT):
            grow = r0 + mt * 128
            t0 = grow % T
            mg_ld = dma(t_g[:, :D], dg[grow:grow + 128, :], waits=[(DV, DV.n)], sem=LG)
            for half in range(2):
                pk = pp[2 + half]
                for kt in range(KS):
                    lhs = sHT[:].rearrange("p (t f) -> p t f", f=4)[:, t0:t0 + 128, 2 * kt + b]
                    pe(lambda e, pk=pk, lhs=lhs, kt=kt, half=half: e.matmul(
                        pk[:, :], lhs,
                        sct[:, kt * D + half * 512:kt * D + half * 512 + 512],
                        start=(kt == 0), stop=False),
                       waits=([(ACS, ACS.n), (DV, m_ct)] if (kt == 0 and half == 0) else ()),
                       inc=False)
                for kt in range(KS):
                    pe(lambda e, pk=pk, kt=kt, mt=mt, half=half: e.matmul(
                        pk[:, :],
                        suT[:, kt * CH + mt * 128:kt * CH + (mt + 1) * 128],
                        sdt[:, kt * D + half * 512:kt * D + half * 512 + 512],
                        start=False, stop=(kt == KS - 1)),
                       waits=([muT, (DV, m_dt)] if (kt == 0 and half == 0) else ()),
                       inc=(kt == KS - 1))
            mY = PE.n
            mt3 = None
            for half in range(2):
                pk = pp[2 + half]
                o = mt * D + half * 512
                mt1 = vec(lambda e, pk=pk, o=o, half=half: e.scalar_tensor_tensor(
                    t_sil[half][:, :512], pk[:, :], 1.0, sxb_[:, o:o + 512],
                    ALU.mult, ALU.subtract),
                    [(PE, mY), (LD, mlxin), (LG, mg_ld), (AC, AC.n)])
                mt2 = vec(lambda e, o=o, half=half: e.tensor_tensor(
                    t_sil[half][:, :512], t_sil[half][:, :512],
                    t_g[:, half * 512:half * 512 + 512],
                    ALU.mult), [(DV, mt1)])
                mt3 = vec(lambda e, o=o, half=half: e.tensor_tensor(
                    sq_[:, o:o + 512], t_sil[half][:, :512],
                    sxb_[:, o:o + 512], ALU.add),
                          [(DV, mt2)])
                prog.wait("tensor", DV, mt3)
            dstv = sqT[:, mt * D:(mt + 1) * D].rearrange("p (k c) -> p k c", c=128)
            srcv = sq_[:, mt * D:(mt + 1) * D].rearrange("p (k c) -> p k c", c=128)
            mygT = dmat(dstv, srcv, [(DV, mt3)], sem=TQ[mt])
            for half in range(2):
                pk = pp[4 + half]
                for kt in range(KD):
                    pe(lambda e, pk=pk, kt=kt, mt=mt, half=half: e.matmul(
                        pk[:, :],
                        sqT[:, mt * D + kt * 128:mt * D + kt * 128 + 128],
                        swgo[:, kt * D + half * 512:kt * D + half * 512 + 512],
                        start=(kt == 0), stop=False),
                       waits=([(TQ[mt], mygT), (DV, m_wo), (DV, m_bo)] if kt == 0 else ()),
                       inc=False)
                pe(lambda e, pk=pk, half=half: e.matmul(
                    pk[:, :], s_one[:, :128], s_bias[:, half * 512:(half + 1) * 512],
                    start=False, stop=True))
            mW = PE.n
            me2 = None
            for half in range(2):
                pk = pp[4 + half]
                o = mt * D + half * 512
                me1 = vec(lambda e, pk=pk, o=o, half=half: e.scalar_tensor_tensor(
                    t_sil[half][:, :512], pk[:, :], 1.0, sxa[:, o:o + 512], ALU.mult, ALU.add),
                    [(PE, mW), (LX1, mlx1)])
                me2 = vec(lambda e, o=o, half=half: e.tensor_tensor(
                    sxa[:, o:o + 512], t_sil[half][:, :512],
                    sxb_[:, o:o + 512], ALU.add),
                          [(DV, me1)])
                prog.wait("tensor", DV, me2)
        mx2 = DV.n
        dma(dx2[r0:r0 + CH, :].rearrange("(m p) d -> p m d", p=128),
            sxa[:].rearrange("p (m d) -> p m d", d=D), waits=[(DV, mx2)], sem=ST)

    ffn_phase(dx2, dy)

    if debug and dwdbg is not None:
        dma(dwdbg, swin[:], sem=ST)
        dma(dht, sHT[:], sem=ST)
    join_all(full=True)

    with nc.Block() as block:
        @block.sync
        def _(sync):
            for f in prog.ops["sync"]:
                f(sync)

        @block.vector
        def _(vector):
            for f in prog.ops["vector"]:
                f(vector)

        @block.scalar
        def _(scalar):
            for f in prog.ops["scalar"]:
                f(scalar)

        @block.tensor
        def _(tensor):
            for f in prog.ops["tensor"]:
                f(tensor)

        @block.gpsimd
        def _(gpsimd):
            for f in prog.ops["gpsimd"]:
                f(gpsimd)

    es.close()
    return nc


_NC_CACHE = {}


def _prep_inputs(L0_attn_x, L0_attn_W_in, L0_attn_b_in, L0_attn_W_gate, L0_attn_b_gate,
                 L0_attn_A, L0_attn_Bm, L0_attn_C, L0_attn_D, L0_attn_W_out, L0_attn_b_out,
                 L0_ffn_w_gate, L0_ffn_w_up, L0_ffn_w_down):
    x = np.ascontiguousarray(np.asarray(L0_attn_x, dtype=np.float32))
    B, T = x.shape[0], x.shape[1]
    shards = [np.ascontiguousarray(x[BPC * c:BPC * (c + 1)].reshape(BPC * T, D))
              for c in range(NCORES)]
    com = {
        "w_in": np.ascontiguousarray(np.asarray(L0_attn_W_in, np.float32)),
        "b_in": np.ascontiguousarray(np.asarray(L0_attn_b_in, np.float32).reshape(1, DS)),
        "w_gate_m": np.ascontiguousarray(np.asarray(L0_attn_W_gate, np.float32)),
        "b_gate_m": np.ascontiguousarray(np.asarray(L0_attn_b_gate, np.float32).reshape(1, D)),
        "a_m": np.ascontiguousarray(np.asarray(L0_attn_A, np.float32)),
        "bm_m": np.ascontiguousarray(np.asarray(L0_attn_Bm, np.float32)),
        "c_t": np.ascontiguousarray(np.asarray(L0_attn_C, np.float32).T),
        "d_t": np.ascontiguousarray(np.asarray(L0_attn_D, np.float32).T),
        "w_out": np.ascontiguousarray(np.asarray(L0_attn_W_out, np.float32)),
        "b_out": np.ascontiguousarray(np.asarray(L0_attn_b_out, np.float32).reshape(1, D)),
        "ffn_wg": np.ascontiguousarray(np.asarray(L0_ffn_w_gate, np.float32)),
        "ffn_wu": np.ascontiguousarray(np.asarray(L0_ffn_w_up, np.float32)),
        "ffn_wd": np.ascontiguousarray(np.asarray(L0_ffn_w_down, np.float32)),
    }
    return shards, com, B, T


def kernel(**inputs):
    shards, com, B, T = _prep_inputs(**inputs)
    if T not in _NC_CACHE:
        _NC_CACHE[T] = build_nc(T)
    nc = _NC_CACHE[T]
    in_maps = [dict(com, x=shards[c]) for c in range(NCORES)]
    res = run_bass_kernel_spmd(nc, in_maps, core_ids=list(range(NCORES)))
    out = np.empty((B, T, D), np.float32)
    for c in range(NCORES):
        out[BPC * c:BPC * (c + 1)] = res.results[c]["y"].reshape(BPC, T, D)
    return out


# version 09
# speedup vs baseline: 1.0879x; 1.0297x over previous
import contextlib
import numpy as np
import concourse.bass as bass
import concourse.mybir as mybir
from concourse.bass_utils import run_bass_kernel_spmd

F32 = mybir.dt.float32
BF16 = mybir.dt.bfloat16
AF = mybir.ActivationFunctionType
ALU = mybir.AluOpType
AX = mybir.AxisListType

NCORES = 8
B_GLOBAL = 16
D = 1024
DS = 256
I = 2816
BPC = B_GLOBAL // NCORES
MID = (0.25, 0.75, 1.25, 1.75, 2.5, 3.5, 5.0)
EPS = 1e-5
KD = D // 128
KS = DS // 128
KI = I // 128

ENGINES = ("sync", "vector", "scalar", "tensor", "gpsimd")


class Sem:
    def __init__(self, nc, es, name):
        self.h = es.enter_context(nc.semaphore(name))
        self.name = name
        self.n = 0


class Prog:
    def __init__(self):
        self.ops = {e: [] for e in ENGINES}
        self.waited = {e: {} for e in ENGINES}

    def emit(self, eng, fn):
        self.ops[eng].append(fn)

    def wait(self, eng, sem, val):
        if val is None or val <= 0:
            return
        if self.waited[eng].get(sem.name, 0) >= val:
            return
        self.waited[eng][sem.name] = val
        self.ops[eng].append(lambda e, s=sem.h, v=val: e.wait_ge(s, v))


def build_nc(T=4096, debug=False):
    TOK = BPC * T
    CH = 512
    NMT = CH // 128
    NCH = TOK // CH
    NS = 512
    SLICES = [(s, min(NS, I - s)) for s in range(0, I, NS)]
    W = 256
    NW = T // W

    nc = bass.Bass("TRN2", target_bir_lowering=False, debug=False)
    es = contextlib.ExitStack()

    dx = nc.dram_tensor("x", [TOK, D], F32, kind="ExternalInput").ap()
    dwin = nc.dram_tensor("w_in", [D, DS], F32, kind="ExternalInput").ap()
    dbin = nc.dram_tensor("b_in", [1, DS], F32, kind="ExternalInput").ap()
    dwgm = nc.dram_tensor("w_gate_m", [D, D], F32, kind="ExternalInput").ap()
    dbgm = nc.dram_tensor("b_gate_m", [1, D], F32, kind="ExternalInput").ap()
    dam = nc.dram_tensor("a_m", [DS, DS], F32, kind="ExternalInput").ap()
    dbm = nc.dram_tensor("bm_m", [DS, DS], F32, kind="ExternalInput").ap()
    dct = nc.dram_tensor("c_t", [DS, D], F32, kind="ExternalInput").ap()
    ddt = nc.dram_tensor("d_t", [DS, D], F32, kind="ExternalInput").ap()
    dwo = nc.dram_tensor("w_out", [D, D], F32, kind="ExternalInput").ap()
    dbo = nc.dram_tensor("b_out", [1, D], F32, kind="ExternalInput").ap()
    dwg = nc.dram_tensor("ffn_wg", [I, D], F32, kind="ExternalInput").ap()
    dwu = nc.dram_tensor("ffn_wu", [I, D], F32, kind="ExternalInput").ap()
    dwd = nc.dram_tensor("ffn_wd", [D, I], F32, kind="ExternalInput").ap()
    dy = nc.dram_tensor("y", [TOK, D], F32, kind="ExternalOutput").ap()
    _ik = "ExternalOutput" if debug else "Internal"
    dx1 = nc.dram_tensor("x1s", [TOK, D], F32, kind=_ik).ap()
    dx2 = nc.dram_tensor("x2s", [TOK, D], F32, kind=_ik).ap()
    dxin = nc.dram_tensor("xins", [TOK, D], F32, kind=_ik).ap()
    dg = nc.dram_tensor("gs", [TOK, D], F32, kind=_ik).ap()
    du = nc.dram_tensor("us", [TOK, DS], BF16, kind=_ik).ap()
    dwgq = nc.dram_tensor("wgqT", [KD, 128, I], BF16, kind=_ik).ap()
    dwuq = nc.dram_tensor("wuqT", [KD, 128, I], BF16, kind=_ik).ap()
    dwdq = nc.dram_tensor("wdqT", [KI, 128, D], BF16, kind=_ik).ap()
    dxq = nc.dram_tensor("xqdbg", [TOK, D], BF16, kind=_ik).ap() if debug else None
    dhq = nc.dram_tensor("hqdbg", [TOK, I], BF16, kind=_ik).ap() if debug else None
    dqt = nc.dram_tensor("qtdbg", [TOK, D], BF16, kind=_ik).ap() if debug else None
    dup = nc.dram_tensor("updbg", [TOK, DS], F32, kind=_ik).ap() if debug else None
    dwdbg = nc.dram_tensor("wdbg", [128, KD * DS], BF16, kind=_ik).ap() if debug else None
    dht = nc.dram_tensor("htdbg", [128, 4 * T], BF16, kind=_ik).ap() if debug else None
    dvb = nc.dram_tensor("vbdbg", [128, 1024], F32, kind=_ik).ap() if debug else None

    def sb(name, shape, dt=BF16):
        return es.enter_context(nc.sbuf_tensor(name, shape, dt))

    swin = sb("swin", [128, KD * DS])
    swgo = sb("swgo", [128, KD * D])
    sct = sb("sct", [128, KS * D])
    sdt = sb("sdt", [128, KS * D])
    sA = sb("sA", [128, 4 * 128])
    sBm = sb("sBm", [128, 4 * 128])
    sHT = sb("sHT", [128, 4 * T])
    swg = [sb(f"swg{i}", [128, KD * NS]) for i in range(2)]
    swu = [sb(f"swu{i}", [128, KD * NS]) for i in range(2)]
    swd = [sb(f"swd{i}", [128, NS]) for i in range(2)]
    sxa = sb("sxa", [128, NMT * D], F32)
    sxb_ = sb("sxb", [128, NMT * D], F32)
    sq_ = sb("sq", [128, NMT * D])
    sqT = sb("sqT", [128, KD * CH])
    su_ = sb("su", [128, NMT * DS])
    suT = sb("suT", [128, KS * CH])
    suTw = [sb("suTw0", [128, KS * 2 * W])]
    s_hqT = sb("s_hqT", [128, KI * CH])
    t_sil = [sb(f"t_sil{i}", [128, NS], F32) for i in range(2)]
    t_hq = [sb(f"t_hq{i}", [128, NS]) for i in range(2)]
    t_a = [sb(f"t_a{i}", [128, NS], F32) for i in range(2)]
    t_g = sb("t_g", [128, D], F32)
    QB = [[sb(f"qb{p}_{i}", [128, NS]) for i in range(6)] for p in range(2)]
    s_bm = [sb(f"s_bm{i}", [128, NS // 16], F32) for i in range(2)]
    s_ri = [sb(f"s_ri{i}", [128, NS // 16], F32) for i in range(2)]
    s_sc = [sb(f"s_sc{i}", [128, NS // 16], F32) for i in range(2)]
    s_bn = sb("s_bn", [128, 12], F32)
    s_mv = sb("s_mv", [128, 2], F32)
    s_rs = sb("s_rs", [128, 1], F32)
    s_nm = sb("s_nm", [128, 1], F32)
    s_ve = sb("s_ve", [128, 1], F32)
    s_one = sb("s_one", [1, 128])
    s_bias = sb("s_bias", [1, D])
    s_bin = sb("s_bin", [1, DS])
    s_thr = sb("s_thr", [128, 8], F32)
    s_wld = sb("s_wld", [128, 1024], F32)
    s_tp = sb("s_tp", [128, 1024])

    pp = [es.enter_context(nc.psum_tensor(f"pp{i}", [128, 512], F32)) for i in range(8)]

    LD = Sem(nc, es, "LD")
    ST = Sem(nc, es, "ST")
    TW = [Sem(nc, es, f"TW{i}") for i in range(2)]
    TD = [Sem(nc, es, f"TD{i}") for i in range(2)]
    TQ = [Sem(nc, es, f"TQ{i}") for i in range(4)]
    TH = [Sem(nc, es, f"TH{i}") for i in range(4)]
    TU = [Sem(nc, es, f"TU{i}") for i in range(2)]
    TP = [Sem(nc, es, f"TP{i}") for i in range(2)]
    PE = Sem(nc, es, "PE")
    DV = Sem(nc, es, "DV")
    AC = Sem(nc, es, "AC")
    ACS = Sem(nc, es, "ACS")
    SG = Sem(nc, es, "SG")
    LX1 = Sem(nc, es, "LX1")
    LG = Sem(nc, es, "LG")
    SWL = Sem(nc, es, "SWL")
    ALL_DMA = [LD, ST, SG, LX1, LG, SWL] + TW + TD + TQ + TH + TU + TP

    prog = Prog()

    def dma(out, in_, waits=(), sem=None):
        sem = sem or LD
        for (s, v) in waits:
            prog.wait("sync", s, v)
        sem.n += 16
        prog.emit("sync", lambda e, o=out, i=in_, s=sem.h: e.dma_start(out=o, in_=i).then_inc(s, 16))
        return sem.n

    def dmat(out, in_, waits=(), sem=None):
        sem = sem or LD
        for (s, v) in waits:
            prog.wait("sync", s, v)
        sem.n += 16
        prog.emit("sync", lambda e, o=out, i=in_, s=sem.h: e.dma_start_transpose(o, i).then_inc(s, 16))
        return sem.n

    def join_all(full=False):
        sems = ALL_DMA if full else [LD, ST, SG, LX1, LG]
        for s in sems:
            prog.wait("sync", s, s.n)
        prog.wait("sync", PE, PE.n)
        prog.wait("sync", DV, DV.n)
        prog.wait("sync", AC, AC.n)
        prog.wait("sync", ACS, ACS.n)

    def _op(eng, sem, fn, waits):
        for (s, v) in waits:
            prog.wait(eng, s, v)
        sem.n += 1
        prog.emit(eng, lambda e, fn=fn, s=sem.h: fn(e).then_inc(s, 1))
        return sem.n

    def vec(fn, waits=()):
        return _op("vector", DV, fn, waits)

    def act(fn, waits=()):
        return _op("scalar", AC, fn, waits)

    def pe(fn, waits=(), inc=True, sem=None):
        for (s, v) in waits:
            prog.wait("tensor", s, v)
        if inc:
            s = sem or PE
            s.n += 1
            prog.emit("tensor", lambda e, fn=fn, sh=s.h: fn(e).then_inc(sh, 1))
            return s.n
        prog.emit("tensor", fn)
        return None

    qset_marks = [None, None]
    _qctr = [0]

    def quantize(src, dst, F, src_ready):
        p = _qctr[0] % 2
        _qctr[0] += 1
        NB = F // 16
        B = QB[p]
        ta, bm, ri, sc = t_a[p], s_bm[p], s_ri[p], s_sc[p]
        if qset_marks[p] is not None:
            pdv, pac = qset_marks[p]
            prog.wait("vector", AC, pac)
            prog.wait("scalar", DV, pdv)
        w = list(src_ready)
        mbm = vec(lambda e: e.tensor_reduce(bm[:, :NB],
                                            src.rearrange("p (a b) -> p a b", b=16),
                                            AX.X, ALU.max, apply_absolute_value=True), w)
        mbm = vec(lambda e: e.tensor_scalar(bm[:, :NB], bm[:, :NB], 6e-12, None, ALU.max),
                  [(DV, mbm)])
        mri = vec(lambda e: e.reciprocal(ri[:, :NB], bm[:, :NB]), [(DV, mbm)])
        mri = vec(lambda e: e.tensor_scalar(ri[:, :NB], ri[:, :NB], 6.0, None, ALU.mult),
                  [(DV, mri)])
        msc = vec(lambda e: e.tensor_scalar(sc[:, :NB], bm[:, :NB], 1.0 / 12.0, None,
                                            ALU.mult), [(DV, mbm)])
        mah = act(lambda e: e.activation(ta[:, :F], src, AF.Abs), w)
        rb = ri[:, :NB].rearrange("p (a o) -> p a o", o=1).broadcast_to([128, NB, 16])
        ma = vec(lambda e: e.tensor_tensor(ta[:, :F].rearrange("p (a b) -> p a b", b=16),
                                           ta[:, :F].rearrange("p (a b) -> p a b", b=16),
                                           rb, ALU.mult), [(DV, mri), (AC, mah)])
        aw = [(DV, ma)]
        m1 = vec(lambda e: e.tensor_scalar(B[0][:, :F], ta[:, :F], MID[0], None, ALU.is_gt), aw)
        m2 = vec(lambda e: e.tensor_scalar(B[1][:, :F], ta[:, :F], MID[1], None, ALU.is_gt), aw)
        mc = [act(lambda e, i=i, j=j: e.activation(B[j][:, :F], ta[:, :F], AF.Sign,
                                                   bias=s_thr[:, i:i + 1]), aw)
              for i, j in ((2, 2), (3, 3), (4, 4), (5, 5))]
        m12 = vec(lambda e: e.tensor_add(B[0][:, :F], B[0][:, :F], B[1][:, :F]), [(DV, m2)])
        m34 = vec(lambda e: e.tensor_add(B[2][:, :F], B[2][:, :F], B[3][:, :F]),
                  [(AC, mc[1])])
        m56 = vec(lambda e: e.tensor_add(B[4][:, :F], B[4][:, :F], B[5][:, :F]),
                  [(AC, mc[3])])
        mc7 = act(lambda e: e.activation(B[5][:, :F], ta[:, :F], AF.Sign,
                                         bias=s_thr[:, 6:7]), aw + [(DV, m56)])
        mv1 = vec(lambda e: e.scalar_tensor_tensor(B[2][:, :F], B[2][:, :F], 0.5,
                                                   B[0][:, :F], ALU.mult, ALU.add),
                  [(DV, m34), (DV, m12)])
        mv2 = vec(lambda e: e.scalar_tensor_tensor(B[4][:, :F], B[5][:, :F], 2.0,
                                                   B[4][:, :F], ALU.mult, ALU.add),
                  [(AC, mc7), (DV, m56)])
        mq2 = vec(lambda e: e.tensor_add(B[2][:, :F], B[2][:, :F], B[4][:, :F]),
                  [(DV, mv2), (DV, mv1)])
        scb = sc[:, :NB].rearrange("p (a o) -> p a o", o=1).broadcast_to([128, NB, 16])
        mt_ = vec(lambda e: e.scalar_tensor_tensor(B[3][:, :F].rearrange("p (a b) -> p a b", b=16),
                                                   B[2][:, :F].rearrange("p (a b) -> p a b", b=16),
                                                   5.0, scb, ALU.add, ALU.mult),
                  [(DV, mq2), (DV, msc)])
        msg = act(lambda e: e.activation(B[1][:, :F], src, AF.Sign), w + [(DV, m12)])
        mq = vec(lambda e: e.tensor_tensor(dst, B[3][:, :F], B[1][:, :F], ALU.mult),
                 [(DV, mt_), (AC, msg)])
        qset_marks[p] = (mq, max(msg, mc7))
        return (DV, mq)

    def layernorm(src, dst, src_ready):
        prog.wait("vector", AC, AC.n)
        prog.wait("scalar", DV, DV.n)
        w = list(src_ready)
        m1 = vec(lambda e: e.bn_stats(s_bn[:, 0:6], src[:, 0:512]), w)
        m1 = vec(lambda e: e.bn_stats(s_bn[:, 6:12], src[:, 512:1024]), w)
        m2 = vec(lambda e: e.bn_aggr(s_mv[:], s_bn[:]), [(DV, m1)])
        m3 = vec(lambda e: e.tensor_scalar(s_ve[:], s_mv[:, 1:2], EPS, None, ALU.add), [(DV, m2)])
        m4 = vec(lambda e: e.reciprocal(s_ve[:], s_ve[:]), [(DV, m3)])
        m5 = act(lambda e: e.activation(s_rs[:], s_ve[:], AF.Sqrt), [(DV, m4)])
        m6 = vec(lambda e: e.tensor_tensor(s_nm[:], s_mv[:, 0:1], s_rs[:], ALU.mult), [(AC, m5)])
        m7 = vec(lambda e: e.tensor_scalar(s_nm[:], s_nm[:], -1.0, None, ALU.mult), [(DV, m6)])
        m8 = act(lambda e: e.activation(dst, src, AF.Identity, bias=s_nm[:], scale=s_rs[:]),
                 [(DV, m7)])
        return (AC, m8)

    vec(lambda e: e.memset(s_one[:], 1.0))
    for i, mv in enumerate(MID):
        vec(lambda e, i=i, mv=mv: e.memset(s_thr[:, i:i + 1], -mv))

    def load_cast(dst_sb, dram, rows, cols):
        last = DV.n
        for kt in range(rows // 128):
            for cc in range(0, cols, 512):
                cl = min(512, cols - cc)
                ml = dma(s_wld[:, :cl], dram[kt * 128:(kt + 1) * 128, cc:cc + cl],
                         waits=[(DV, last)], sem=SWL)
                last = vec(lambda e, kt=kt, cc=cc, cl=cl: e.tensor_copy(
                    dst_sb[:, kt * cols + cc:kt * cols + cc + cl], s_wld[:, :cl]),
                    [(SWL, ml)])
        return last

    m_win = load_cast(swin, dwin, D, DS)
    m_wg = load_cast(swgo, dwgm, D, D)
    m_ct = load_cast(sct, dct, DS, D)
    m_dt = load_cast(sdt, ddt, DS, D)

    def load_quad(dst_sb, dram):
        last = DV.n
        for kt in range(2):
            ml = dma(s_wld[:, :DS], dram[kt * 128:(kt + 1) * 128, :],
                     waits=[(DV, last)], sem=SWL)
            for mt in range(2):
                last = vec(lambda e, kt=kt, mt=mt: e.tensor_copy(
                    dst_sb[:, (2 * kt + mt) * 128:(2 * kt + mt) * 128 + 128],
                    s_wld[:, mt * 128:(mt + 1) * 128]), [(SWL, ml)])
        return last

    m_A = load_quad(sA, dam)
    m_Bm = load_quad(sBm, dbm)
    m_binf = dma(s_wld[0:1, :DS], dbin, waits=[(DV, m_Bm)], sem=SWL)
    m_bin = vec(lambda e: e.tensor_copy(s_bin[:], s_wld[0:1, :DS]), [(SWL, m_binf)])
    m_bgf = dma(s_wld[0:1, :], dbgm, waits=[(DV, m_bin)], sem=SWL)
    m_bg = vec(lambda e: e.tensor_copy(s_bias[:], s_wld[0:1, :]), [(SWL, m_bgf)])

    def prep_ffn(dram_w, dram_q, rows, cols):
        hist = [((DV, DV.n), (AC, AC.n)), ((DV, DV.n), (AC, AC.n))]
        st_mark = {0: 0, 1: 0}
        tp_mark = {0: 0, 1: 0}
        it = 0
        for rt in range(rows // 128):
            for cc in range(0, cols, 512):
                cl = min(512, cols - cc)
                par = it % 2
                wo = par * 512
                ml = dma(s_wld[:, wo:wo + cl], dram_w[rt * 128:(rt + 1) * 128, cc:cc + cl],
                         waits=list(hist[-2]), sem=SWL)
                prog.wait("vector", TP[par], tp_mark[par])
                mq = quantize(s_wld[:, wo:wo + cl], t_hq[par][:, :cl], cl, [(SWL, ml)])
                mtp = dmat(s_tp[:, wo:wo + cl].rearrange("p (k c) -> p k c", c=128),
                           t_hq[par][:, :cl].rearrange("p (k c) -> p k c", c=128),
                           [mq, (ST, st_mark[par])], sem=TP[par])
                for j in range(cl // 128):
                    dma(dram_q[cc // 128 + j, :, rt * 128:(rt + 1) * 128],
                        s_tp[:, wo + j * 128:wo + (j + 1) * 128],
                        waits=[(TP[par], mtp)], sem=ST)
                st_mark[par] = ST.n
                tp_mark[par] = TP[par].n
                hist.append((mq, (AC, AC.n)))
                it += 1
        return hist[-1][0]

    prep_ffn(dwg, dwgq, I, D)
    prep_ffn(dwu, dwuq, I, D)
    prep_ffn(dwd, dwdq, D, I)

    def ffn_phase(dsrc, ddst, dbg=False):
        prog.wait("vector", TP[0], TP[0].n)
        prog.wait("vector", TP[1], TP[1].n)
        for c in range(NCH):
            r0 = c * CH
            join_all()
            mlx = dma(sxa[:].rearrange("p (m d) -> p m d", d=D),
                      dsrc[r0:r0 + CH, :].rearrange("(m p) d -> p m d", p=128))
            mqTs = []
            for mt in range(NMT):
                mln = layernorm(sxa[:, mt * D:(mt + 1) * D], sxb_[:, mt * D:(mt + 1) * D],
                                [(LD, mlx)])
                mqa = quantize(sxb_[:, mt * D:mt * D + 512], sq_[:, mt * D:mt * D + 512],
                               512, [mln])
                mqb = quantize(sxb_[:, mt * D + 512:mt * D + 1024],
                               sq_[:, mt * D + 512:mt * D + 1024], 512, [mln])
                dstv = sqT[:, mt * D:(mt + 1) * D].rearrange("p (k c) -> p k c", c=128)
                srcv = sq_[:, mt * D:(mt + 1) * D].rearrange("p (k c) -> p k c", c=128)
                mqTs.append((TQ[mt], dmat(dstv, srcv, [mqa, mqb], sem=TQ[mt])))
                if dbg and dxq is not None:
                    dma(dxq[r0 + mt * 128:r0 + (mt + 1) * 128, :],
                        sq_[:, mt * D:(mt + 1) * D], waits=[mqa, mqb], sem=ST)
            pe_after_slice = {}
            guards = {0: (0, 0), 1: (0, 0)}
            it = 0
            for si, (soff, slen) in enumerate(SLICES):
                par = si % 2
                prog.wait("sync", PE, pe_after_slice.get(si - 2, 0))
                dma(swg[par][:].rearrange("p (k n) -> p k n", n=NS)[:, :, :slen],
                    dwgq[:, :, soff:soff + slen].rearrange("k p n -> p k n"), sem=TW[par])
                mwu = dma(swu[par][:].rearrange("p (k n) -> p k n", n=NS)[:, :, :slen],
                          dwuq[:, :, soff:soff + slen].rearrange("k p n -> p k n"), sem=TW[par])
                for mt in range(NMT):
                    bp = it % 2
                    pg = pp[0] if bp == 0 else pp[6]
                    pu = pp[1] if bp == 0 else pp[7]
                    prog.wait("tensor", AC, guards[bp][0])
                    prog.wait("tensor", DV, guards[bp][1])
                    for kt in range(KD):
                        pe(lambda e, kt=kt, mt=mt, par=par, slen=slen, pg=pg: e.matmul(
                            pg[:, :slen],
                            sqT[:, mt * D + kt * 128:mt * D + kt * 128 + 128],
                            swg[par][:, kt * NS:kt * NS + slen],
                            start=(kt == 0), stop=(kt == KD - 1)),
                           waits=([mqTs[mt], (TW[par], mwu)] if kt == 0 else ()),
                           inc=(kt == KD - 1))
                    mpg = PE.n
                    for kt in range(KD):
                        pe(lambda e, kt=kt, mt=mt, par=par, slen=slen, pu=pu: e.matmul(
                            pu[:, :slen],
                            sqT[:, mt * D + kt * 128:mt * D + kt * 128 + 128],
                            swu[par][:, kt * NS:kt * NS + slen],
                            start=(kt == 0), stop=(kt == KD - 1)),
                           inc=(kt == KD - 1))
                    mpu = PE.n
                    msil = act(lambda e, slen=slen, pg=pg, bp=bp: e.activation(
                        t_sil[bp][:, :slen], pg[:, :slen], AF.Silu),
                        [(PE, mpg), (DV, guards[bp][1])])
                    mh = vec(lambda e, slen=slen, pu=pu, bp=bp: e.tensor_tensor(
                        t_sil[bp][:, :slen], t_sil[bp][:, :slen], pu[:, :slen], ALU.mult),
                        [(AC, msil), (PE, mpu)])
                    mhq = quantize(t_sil[bp][:, :slen], t_hq[bp][:, :slen], slen, [(DV, mh)])
                    guards[bp] = (msil, mhq[1])
                    dstv = s_hqT[:, mt * I + soff:mt * I + soff + slen].rearrange(
                        "p (k c) -> p k c", c=128)
                    srcv = t_hq[bp][:, :slen].rearrange("p (k c) -> p k c", c=128)
                    dmat(dstv, srcv, [mhq], sem=TH[mt])
                    if dbg and dhq is not None:
                        dma(dhq[r0 + mt * 128:r0 + (mt + 1) * 128, soff:soff + slen],
                            t_hq[bp][:, :slen], waits=[mhq], sem=ST)
                    it += 1
                pe_after_slice[si] = PE.n
            th_marks = [(TH[mt], TH[mt].n) for mt in range(NMT)]
            for half in range(2):
                mwd_prev = {-2: PE.n, -1: PE.n}
                for ktg in range(KI):
                    par = ktg % 2
                    prog.wait("sync", PE, mwd_prev[ktg - 2])
                    mwd = dma(swd[par][:], dwdq[ktg, :, half * 512:(half + 1) * 512],
                              sem=TD[par])
                    for mt in range(NMT):
                        pe(lambda e, ktg=ktg, mt=mt, par=par: e.matmul(
                            pp[2 + mt][:, :],
                            s_hqT[:, mt * I + ktg * 128:mt * I + ktg * 128 + 128],
                            swd[par][:],
                            start=(ktg == 0), stop=(ktg == KI - 1)),
                           waits=(th_marks + [(TD[par], mwd)] if mt == 0 else ()),
                           inc=(mt == NMT - 1))
                    mwd_prev[ktg] = PE.n
                mdn = PE.n
                for mt in range(NMT):
                    o = mt * D + half * 512
                    mev = vec(lambda e, mt=mt, o=o: e.scalar_tensor_tensor(
                        sxa[:, o:o + 512], pp[2 + mt][:, :], 0.5, sxa[:, o:o + 512],
                        ALU.mult, ALU.add), [(PE, mdn)])
                    prog.wait("tensor", DV, mev)
            mst = DV.n
            dma(ddst[r0:r0 + CH, :].rearrange("(m p) d -> p m d", p=128),
                sxa[:].rearrange("p (m d) -> p m d", d=D), waits=[(DV, mst)])

    ffn_phase(dx, dx1, dbg=debug)

    for c in range(NCH):
        r0 = c * CH
        join_all()
        mlx = dma(sxa[:].rearrange("p (m d) -> p m d", d=D),
                  dx1[r0:r0 + CH, :].rearrange("(m p) d -> p m d", p=128))
        for mt in range(NMT):
            mln = layernorm(sxa[:, mt * D:(mt + 1) * D], sxb_[:, mt * D:(mt + 1) * D],
                            [(LD, mlx)])
            mstx = dma(dxin[r0 + mt * 128:r0 + (mt + 1) * 128, :],
                       sxb_[:, mt * D:(mt + 1) * D], waits=[mln], sem=ST)
            mcb = vec(lambda e, mt=mt: e.tensor_copy(sq_[:, mt * D:(mt + 1) * D],
                                                     sxb_[:, mt * D:(mt + 1) * D]), [mln])
            dstv = sqT[:, mt * D:(mt + 1) * D].rearrange("p (k c) -> p k c", c=128)
            srcv = sq_[:, mt * D:(mt + 1) * D].rearrange("p (k c) -> p k c", c=128)
            mqT = dmat(dstv, srcv, [(DV, mcb)], sem=TQ[mt])
            if debug and dqt is not None:
                dma(dqt[r0 + mt * 128:r0 + (mt + 1) * 128, :], sqT[:, mt * D:(mt + 1) * D],
                    waits=[(TQ[mt], mqT)], sem=ST)
            for kt in range(KD):
                pe(lambda e, kt=kt, mt=mt: e.matmul(
                    pp[0][:, :DS],
                    sqT[:, mt * D + kt * 128:mt * D + kt * 128 + 128],
                    swin[:, kt * DS:(kt + 1) * DS], start=(kt == 0), stop=False),
                   waits=([(TQ[mt], mqT), (DV, m_win), (DV, m_bin)] if kt == 0 else ()),
                   inc=False)
            mu_ = pe(lambda e: e.matmul(pp[0][:, :DS], s_one[:, :128], s_bin[:],
                                        start=False, stop=True))
            dbgw = []
            if debug and dup is not None:
                mevd = act(lambda e: e.activation(t_sil[0][:, :DS], pp[0][:, :DS], AF.Identity),
                           [(PE, mu_)])
                dma(dup[r0 + mt * 128:r0 + (mt + 1) * 128, :], t_sil[0][:, :DS],
                    waits=[(AC, mevd)], sem=ST)
                prog.wait("tensor", AC, mevd)
                dbgw = [(AC, mevd)]
            mev = vec(lambda e, mt=mt: e.tensor_copy(su_[:, mt * DS:(mt + 1) * DS],
                                                     pp[0][:, :DS]), [(PE, mu_)] + dbgw)
            prog.wait("tensor", DV, mev)
            dma(du[r0 + mt * 128:r0 + (mt + 1) * 128, :], su_[:, mt * DS:(mt + 1) * DS],
                waits=[(DV, mev)], sem=ST)
            for half in range(2):
                for kt in range(KD):
                    pe(lambda e, kt=kt, mt=mt, half=half: e.matmul(
                        pp[2 + half][:, :],
                        sqT[:, mt * D + kt * 128:mt * D + kt * 128 + 128],
                        swgo[:, kt * D + half * 512:kt * D + half * 512 + 512],
                        start=(kt == 0), stop=False),
                       waits=([(DV, m_wg), (DV, m_bg)] if kt == 0 else ()), inc=False)
                pe(lambda e, half=half: e.matmul(pp[2 + half][:, :], s_one[:, :128],
                                                 s_bias[:, half * 512:(half + 1) * 512],
                                                 start=False, stop=True))
            mg_ = PE.n
            mgs = None
            for half in range(2):
                mgs = act(lambda e, half=half: e.activation(
                    t_g[:, half * 512:half * 512 + 512],
                    pp[2 + half][:, :], AF.Sigmoid),
                    [(PE, mg_), (DV, DV.n)])
            prog.wait("tensor", AC, mgs)
            mgst = dma(dg[r0 + mt * 128:r0 + (mt + 1) * 128, :], t_g[:, :D],
                       waits=[(AC, mgs)], sem=SG)
            prog.wait("scalar", SG, mgst)

    def load_uT_window(w, buf):
        t0 = w * W
        prog.wait("sync", PE, PE.n)
        last = None
        for kt in range(KS):
            for b in range(BPC):
                for tb in range(W // 128):
                    srcv = du[b * T + t0 + tb * 128:b * T + t0 + (tb + 1) * 128,
                              kt * 128:(kt + 1) * 128]
                    dstc = kt * (BPC * W) + b * W + tb * 128
                    last = dmat(suTw[buf][:, dstc:dstc + 128], srcv, sem=TU[buf])
        return (TU[buf], last)

    def vfill(pA, pB, muT, buf, extra=()):
        for pk, parity in ((pA, 0), (pB, 1)):
            for mt in range(2):
                for kt in range(KS):
                    view = suTw[buf][:, kt * (BPC * W):(kt + 1) * (BPC * W)].rearrange(
                        "p (b te two) -> p te b two", b=BPC, two=2)
                    rhs = view[:, :, :, parity]
                    last = (parity == 1 and mt == 1 and kt == KS - 1)
                    pe(lambda e, pk=pk, mt=mt, kt=kt, rhs=rhs: e.matmul(
                        pk[:, mt * 256:(mt + 1) * 256],
                        sBm[:, (2 * kt + mt) * 128:(2 * kt + mt) * 128 + 128],
                        rhs, start=(mt == 0 and kt == 0), stop=(mt == 1 and kt == KS - 1)),
                       waits=(([muT, (DV, m_Bm)] + list(extra))
                              if (mt == 0 and kt == 0 and parity == 0) else ()),
                       inc=last)

    join_all(full=True)
    m_uT = load_uT_window(0, 0)
    vfill(pp[4], pp[5], m_uT, 0)
    m_vf0 = PE.n
    if debug and dvb is not None:
        mdbg = []
        for bi, pk in enumerate((pp[4], pp[5])):
            mdbg.append(act(lambda e, pk=pk, bi=bi: e.activation(
                sxa[:, bi * 512:(bi + 1) * 512], pk[:, :], AF.Identity), [(PE, m_vf0)]))
        dma(dvb[:, 0:512], sxa[:, 0:512], waits=[(AC, mdbg[0])], sem=ST)
        dma(dvb[:, 512:1024], sxa[:, 512:1024], waits=[(AC, mdbg[1])], sem=ST)
        prog.wait("tensor", AC, mdbg[-1])
        for s in ALL_DMA:
            prog.wait("tensor", s, s.n) if False else None
        prog.wait("tensor", ST, ST.n)

    for w in range(NW):
        cur = (pp[4], pp[5]) if w % 2 == 0 else (pp[6], pp[7])
        nxt = (pp[6], pp[7]) if w % 2 == 0 else (pp[4], pp[5])
        m_uTn = load_uT_window(w + 1, 0) if w + 1 < NW else None
        for tl in range(W):
            t = w * W + tl
            bank = cur[tl % 2]
            c0 = tl // 2
            if t == 0:
                mpe = m_vf0
            else:
                hsrc = 4 * (t - 1)
                for kt in range(KS):
                    for mt in range(2):
                        outv = bank[:, mt * 256 + 2 * c0:mt * 256 + 2 * c0 + 2]
                        pe(lambda e, outv=outv, kt=kt, mt=mt, hsrc=hsrc: e.matmul(
                            outv,
                            sA[:, (2 * kt + mt) * 128:(2 * kt + mt) * 128 + 128],
                            sHT[:, hsrc + 2 * kt:hsrc + 2 * kt + 2],
                            start=False, stop=(kt == KS - 1)),
                           waits=([(DV, m_A)] if t == 1 and kt == 0 and mt == 0 else ()),
                           inc=(kt == KS - 1 and mt == 1))
                mpe = PE.n
            srcv = bank[:].rearrange("p (m r) -> p m r", m=2)[:, :, 2 * c0:2 * c0 + 2]
            dstv = sHT[:, 4 * t:4 * t + 4].rearrange("p (k b) -> p k b", k=2)
            ma = _op("scalar", ACS,
                     lambda e, s=srcv, d=dstv: e.activation(d, s, AF.Tanh),
                     [(PE, mpe)])
            prog.wait("tensor", ACS, ma)
            if tl == 16 and w + 1 < NW:
                vfill(nxt[0], nxt[1], m_uTn, 0)

    prog.wait("sync", PE, PE.n)
    m_wo = load_cast(swgo, dwo, D, D)
    m_bof = dma(s_wld[0:1, :], dbo, waits=[(DV, DV.n)], sem=SWL)
    m_bo = vec(lambda e: e.tensor_copy(s_bias[:], s_wld[0:1, :]), [(SWL, m_bof)])
    for c in range(NCH):
        r0 = c * CH
        b = r0 // T
        join_all()
        mlxin = dma(sxb_[:].rearrange("p (m d) -> p m d", d=D),
                    dxin[r0:r0 + CH, :].rearrange("(m p) d -> p m d", p=128))
        mlx1 = dma(sxa[:].rearrange("p (m d) -> p m d", d=D),
                   dx1[r0:r0 + CH, :].rearrange("(m p) d -> p m d", p=128), sem=LX1)
        muT = None
        for kt in range(KS):
            for tb in range(NMT):
                srcv = du[r0 + tb * 128:r0 + (tb + 1) * 128, kt * 128:(kt + 1) * 128]
                muT = dmat(suT[:, kt * CH + tb * 128:kt * CH + (tb + 1) * 128], srcv,
                           sem=TU[0])
        muT = (TU[0], muT)
        for mt in range(NMT):
            grow = r0 + mt * 128
            t0 = grow % T
            mg_ld = dma(t_g[:, :D], dg[grow:grow + 128, :], waits=[(DV, DV.n)], sem=LG)
            for half in range(2):
                pk = pp[2 + half]
                for kt in range(KS):
                    lhs = sHT[:].rearrange("p (t f) -> p t f", f=4)[:, t0:t0 + 128, 2 * kt + b]
                    pe(lambda e, pk=pk, lhs=lhs, kt=kt, half=half: e.matmul(
                        pk[:, :], lhs,
                        sct[:, kt * D + half * 512:kt * D + half * 512 + 512],
                        start=(kt == 0), stop=False),
                       waits=([(ACS, ACS.n), (DV, m_ct)] if (kt == 0 and half == 0) else ()),
                       inc=False)
                for kt in range(KS):
                    pe(lambda e, pk=pk, kt=kt, mt=mt, half=half: e.matmul(
                        pk[:, :],
                        suT[:, kt * CH + mt * 128:kt * CH + (mt + 1) * 128],
                        sdt[:, kt * D + half * 512:kt * D + half * 512 + 512],
                        start=False, stop=(kt == KS - 1)),
                       waits=([muT, (DV, m_dt)] if (kt == 0 and half == 0) else ()),
                       inc=(kt == KS - 1))
            mY = PE.n
            mt3 = None
            for half in range(2):
                pk = pp[2 + half]
                o = mt * D + half * 512
                mt1 = vec(lambda e, pk=pk, o=o, half=half: e.scalar_tensor_tensor(
                    t_sil[half][:, :512], pk[:, :], 1.0, sxb_[:, o:o + 512],
                    ALU.mult, ALU.subtract),
                    [(PE, mY), (LD, mlxin), (LG, mg_ld), (AC, AC.n)])
                mt2 = vec(lambda e, o=o, half=half: e.tensor_tensor(
                    t_sil[half][:, :512], t_sil[half][:, :512],
                    t_g[:, half * 512:half * 512 + 512],
                    ALU.mult), [(DV, mt1)])
                mt3 = vec(lambda e, o=o, half=half: e.tensor_tensor(
                    sq_[:, o:o + 512], t_sil[half][:, :512],
                    sxb_[:, o:o + 512], ALU.add),
                          [(DV, mt2)])
                prog.wait("tensor", DV, mt3)
            dstv = sqT[:, mt * D:(mt + 1) * D].rearrange("p (k c) -> p k c", c=128)
            srcv = sq_[:, mt * D:(mt + 1) * D].rearrange("p (k c) -> p k c", c=128)
            mygT = dmat(dstv, srcv, [(DV, mt3)], sem=TQ[mt])
            for half in range(2):
                pk = pp[4 + half]
                for kt in range(KD):
                    pe(lambda e, pk=pk, kt=kt, mt=mt, half=half: e.matmul(
                        pk[:, :],
                        sqT[:, mt * D + kt * 128:mt * D + kt * 128 + 128],
                        swgo[:, kt * D + half * 512:kt * D + half * 512 + 512],
                        start=(kt == 0), stop=False),
                       waits=([(TQ[mt], mygT), (DV, m_wo), (DV, m_bo)] if kt == 0 else ()),
                       inc=False)
                pe(lambda e, pk=pk, half=half: e.matmul(
                    pk[:, :], s_one[:, :128], s_bias[:, half * 512:(half + 1) * 512],
                    start=False, stop=True))
            mW = PE.n
            me2 = None
            for half in range(2):
                pk = pp[4 + half]
                o = mt * D + half * 512
                me1 = vec(lambda e, pk=pk, o=o, half=half: e.scalar_tensor_tensor(
                    t_sil[half][:, :512], pk[:, :], 1.0, sxa[:, o:o + 512], ALU.mult, ALU.add),
                    [(PE, mW), (LX1, mlx1)])
                me2 = vec(lambda e, o=o, half=half: e.tensor_tensor(
                    sxa[:, o:o + 512], t_sil[half][:, :512],
                    sxb_[:, o:o + 512], ALU.add),
                          [(DV, me1)])
                prog.wait("tensor", DV, me2)
        mx2 = DV.n
        dma(dx2[r0:r0 + CH, :].rearrange("(m p) d -> p m d", p=128),
            sxa[:].rearrange("p (m d) -> p m d", d=D), waits=[(DV, mx2)], sem=ST)

    ffn_phase(dx2, dy)

    if debug and dwdbg is not None:
        dma(dwdbg, swin[:], sem=ST)
        dma(dht, sHT[:], sem=ST)
    join_all(full=True)

    with nc.Block() as block:
        @block.sync
        def _(sync):
            for f in prog.ops["sync"]:
                f(sync)

        @block.vector
        def _(vector):
            for f in prog.ops["vector"]:
                f(vector)

        @block.scalar
        def _(scalar):
            for f in prog.ops["scalar"]:
                f(scalar)

        @block.tensor
        def _(tensor):
            for f in prog.ops["tensor"]:
                f(tensor)

        @block.gpsimd
        def _(gpsimd):
            for f in prog.ops["gpsimd"]:
                f(gpsimd)

    es.close()
    return nc


_NC_CACHE = {}


def _prep_inputs(L0_attn_x, L0_attn_W_in, L0_attn_b_in, L0_attn_W_gate, L0_attn_b_gate,
                 L0_attn_A, L0_attn_Bm, L0_attn_C, L0_attn_D, L0_attn_W_out, L0_attn_b_out,
                 L0_ffn_w_gate, L0_ffn_w_up, L0_ffn_w_down):
    x = np.ascontiguousarray(np.asarray(L0_attn_x, dtype=np.float32))
    B, T = x.shape[0], x.shape[1]
    shards = [np.ascontiguousarray(x[BPC * c:BPC * (c + 1)].reshape(BPC * T, D))
              for c in range(NCORES)]
    com = {
        "w_in": np.ascontiguousarray(np.asarray(L0_attn_W_in, np.float32)),
        "b_in": np.ascontiguousarray(np.asarray(L0_attn_b_in, np.float32).reshape(1, DS)),
        "w_gate_m": np.ascontiguousarray(np.asarray(L0_attn_W_gate, np.float32)),
        "b_gate_m": np.ascontiguousarray(np.asarray(L0_attn_b_gate, np.float32).reshape(1, D)),
        "a_m": np.ascontiguousarray(np.asarray(L0_attn_A, np.float32)),
        "bm_m": np.ascontiguousarray(np.asarray(L0_attn_Bm, np.float32)),
        "c_t": np.ascontiguousarray(np.asarray(L0_attn_C, np.float32).T),
        "d_t": np.ascontiguousarray(np.asarray(L0_attn_D, np.float32).T),
        "w_out": np.ascontiguousarray(np.asarray(L0_attn_W_out, np.float32)),
        "b_out": np.ascontiguousarray(np.asarray(L0_attn_b_out, np.float32).reshape(1, D)),
        "ffn_wg": np.ascontiguousarray(np.asarray(L0_ffn_w_gate, np.float32)),
        "ffn_wu": np.ascontiguousarray(np.asarray(L0_ffn_w_up, np.float32)),
        "ffn_wd": np.ascontiguousarray(np.asarray(L0_ffn_w_down, np.float32)),
    }
    return shards, com, B, T


def kernel(**inputs):
    shards, com, B, T = _prep_inputs(**inputs)
    if T not in _NC_CACHE:
        _NC_CACHE[T] = build_nc(T)
    nc = _NC_CACHE[T]
    in_maps = [dict(com, x=shards[c]) for c in range(NCORES)]
    res = run_bass_kernel_spmd(nc, in_maps, core_ids=list(range(NCORES)))
    out = np.empty((B, T, D), np.float32)
    for c in range(NCORES):
        out[BPC * c:BPC * (c + 1)] = res.results[c]["y"].reshape(BPC, T, D)
    return out
